# version 08
# baseline (speedup 1.0000x reference)
import numpy as np
from contextlib import ExitStack

import concourse.bass as bass
import concourse.mybir as mybir
from concourse.bass_utils import run_bass_kernel_spmd

F32 = mybir.dt.float32
BF16 = mybir.dt.bfloat16
AF = mybir.ActivationFunctionType
ALU = mybir.AluOpType
AX = mybir.AxisListType

NCORES = 8
T = 16384
DM = 1024
DIN = 2048
DSL = DIN // NCORES
NST = 16
DTR = 64
TOWN = T // NCORES
BLK = 128
NBLK = TOWN // BLK
CH = 1024
NCH = T // CH
CCH = 2048
EPS = 1e-5
RD = 1.0 / DM

ENGINES = ("sync", "vector", "scalar", "tensor", "gpsimd")


class Step:
    __slots__ = ("eng", "fn", "deps", "idx", "seq", "n_dma", "coll", "tag",
                 "semref", "extra_waits")

    def __init__(self, eng, fn, deps, n_dma=0, coll=None, tag=""):
        self.eng, self.fn, self.deps = eng, fn, deps
        self.n_dma, self.coll, self.tag = n_dma, coll, tag
        self.idx = self.seq = None
        self.semref = None
        self.extra_waits = None


NDMASEM = 32


class Builder:
    def __init__(self):
        self.steps = []
        self.eng_count = {e: 0 for e in ENGINES}
        self.dma_step_count = {e: 0 for e in ENGINES}
        self.dma_sem_total = {(e, i): 0 for e in ENGINES
                              for i in range(NDMASEM)}
        self.coll_names = []
        self.bufs = {}

    def buf(self, name):
        if name not in self.bufs:
            self.bufs[name] = [None, []]
        return self.bufs[name]

    def _mk(self, eng, fn, deps, n_dma, coll, tag, reads, writes):
        deps = [d for d in deps if d is not None]
        cname = None
        if coll:
            cname = f"cc{len(self.coll_names)}"
            self.coll_names.append(cname)
        s = Step(eng, fn, deps, n_dma=n_dma, coll=cname, tag=tag)
        s.idx = len(self.steps)
        if n_dma:
            i = self.dma_step_count[eng] % NDMASEM
            self.dma_step_count[eng] += 1
            prev = self.dma_sem_total[(eng, i)]
            self.dma_sem_total[(eng, i)] = prev + n_dma * 16
            s.seq = (i, prev, prev + n_dma * 16)
        elif cname is None:
            self.eng_count[eng] += 1
            s.seq = self.eng_count[eng]
        else:
            s.seq = 1
        self.steps.append(s)
        for b in reads:
            bb = self.buf(b)
            if bb[0] is not None:
                s.deps.append(bb[0])
            bb[1].append(s)
        for b in writes:
            bb = self.buf(b)
            if bb[0] is not None and bb[0] is not s:
                s.deps.append(bb[0])
            s.deps.extend(x for x in bb[1] if x is not s)
            bb[0], bb[1] = s, []
        return s

    def dma(self, eng, pairs, deps=(), tag="", reads=(), writes=()):
        semref = {}

        def fn(e, pairs=pairs, semref=semref):
            last = None
            for dst, src in pairs:
                last = e.dma_start(out=dst, in_=src).then_inc(semref["s"], 16)
            return last
        s = self._mk(eng, fn, list(deps), len(pairs), False, tag, reads,
                     writes)
        s.semref = semref
        return s

    def comp(self, eng, fn, deps=(), tag="", reads=(), writes=(), coll=False):
        return self._mk(eng, fn, list(deps), 0, coll, tag, reads, writes)

    def barrier(self):
        lasts = []
        for e in ENGINES:
            es_ = [s for s in self.steps if s.eng == e]
            if es_:
                lasts.append(es_[-1])
        extra = {(e, "dma", i): tot
                 for (e, i), tot in self.dma_sem_total.items() if tot > 0}
        bsteps = []
        for e in ENGINES:
            s = self._mk(e, lambda eng: eng.nop(), list(lasts), 0, False,
                         f"bar_{e}", (), ())
            s.extra_waits = dict(extra)
            bsteps.append(s)
        for name, bb in self.bufs.items():
            bb[0], bb[1] = bsteps[0], []
        return bsteps

    def emit_for(self, sems, coll_sems, eng_name, eng):
        hw = {}
        for s in self.steps:
            if s.eng != eng_name:
                continue
            waits = {}
            for d in s.deps:
                if d.coll is not None:
                    key = ("coll", d.coll)
                    val = d.seq
                elif d.n_dma:
                    key = (d.eng, "dma", d.seq[0])
                    val = d.seq[2]
                else:
                    key = (d.eng, "seq")
                    val = d.seq
                waits[key] = max(waits.get(key, 0), val)
            if s.n_dma and s.seq[1] > 0:
                k = (s.eng, "dma", s.seq[0])
                waits[k] = max(waits.get(k, 0), s.seq[1])
            if s.extra_waits:
                for k, v in s.extra_waits.items():
                    waits[k] = max(waits.get(k, 0), v)
            for key in sorted(waits, key=repr):
                if waits[key] <= hw.get(key, 0):
                    continue
                hw[key] = waits[key]
                if key[0] == "coll":
                    eng.wait_ge(coll_sems[key[1]], waits[key])
                else:
                    eng.wait_ge(sems[key], waits[key])
            if s.semref is not None:
                s.semref["s"] = sems[(s.eng, "dma", s.seq[0])]
            last = s.fn(eng)
            if s.coll is not None:
                last.then_inc(coll_sems[s.coll])
            elif s.n_dma:
                pass
            else:
                if last is None:
                    last = eng.nop()
                last.then_inc(sems[(s.eng, "seq")], 1)


def dap(handle, offset, dims):
    return bass.AP(handle, offset, dims)


def build_program(debug_names=()):
    nc = bass.Bass("TRN2", target_bir_lowering=False)
    sc = Builder()
    dma, comp = sc.dma, sc.comp

    def din(name, shape, dt=F32):
        return nc.dram_tensor(name, list(shape), dt, kind="ExternalInput")

    def dint(name, shape, dt=F32):
        return nc.dram_tensor(name, list(shape), dt, kind="Internal")

    x_own = din("x_own", (TOWN, DM))
    gamma_t = din("gamma_t", (128, 8))
    beta_t = din("beta_t", (128, 8))
    in_w_xs = din("in_w_xs", (DM, DSL))
    in_w_z = din("in_w_z", (DM, DIN))
    ident_f = din("ident_f", (128, 128))
    ident_b = din("ident_b", (128, 128), BF16)
    ones_f = din("ones_f", (1, 128))
    ind32_f = din("ind32_f", (32, 32 * 128))
    conv_w_t = [din(f"conv_w_t{d}", (128, 2, 4)) for d in range(2)]
    conv_b_t = [din(f"conv_b_t{d}", (128, 2)) for d in range(2)]
    dskip_t = [din(f"dskip_t{d}", (128, 2)) for d in range(2)]
    dtb_t = [din(f"dtb_t{d}", (128, 2)) for d in range(2)]
    alog_t = [din(f"alog_t{d}", (128, 2, NST)) for d in range(2)]
    xproj_w = [din(f"xproj_w{d}", (DSL, 96)) for d in range(2)]
    dt_w = [din(f"dt_w{d}", (DTR, DSL)) for d in range(2)]
    merge_w_p = din("merge_w_p", (16 * 128, 32 * 128))
    out_w = din("out_w", (DIN, DM))
    gw1 = din("gw1", (DM, 2048))
    gb1_t = din("gb1_t", (128, 16))
    gw2 = din("gw2", (2048, 8))
    gb2c = din("gb2c", (8, 1))
    ws1 = din("ws1", (DM, 2048))
    bs1_t = din("bs1_t", (128, 16))
    ws2 = din("ws2", (2048, DM))
    bs2_t = din("bs2_t", (128, 8))
    we1 = din("we1", (8, DM, 2048))
    be1_t = din("be1_t", (8, 128, 16))
    we2 = din("we2", (8, 2048, DM))
    be2 = din("be2", (8, DM))

    out_own = nc.dram_tensor("out_own", [TOWN, DM], F32, kind="ExternalOutput")

    XSW = T + 8
    internals = {}

    def idram(name, shape, dt=F32):
        h = dint(name, shape, dt)
        internals[name] = (h, shape, dt)
        return h

    ag_h2_in = idram("ag_h2_in", (DM, TOWN), BF16)
    ag_h2_out = idram("ag_h2_out", (NCORES * DM, TOWN), BF16)
    ar_in = [idram(f"ar_in{d}", (96, T)) for d in range(2)]
    ar_out = [idram(f"ar_out{d}", (96, T)) for d in range(2)]
    ag_y_in = [idram(f"ag_y_in{c}", (2 * DSL, CH), BF16) for c in range(NCH)]
    ag_y_out = [idram(f"ag_y_out{c}", (NCORES * 2 * DSL, CH), BF16)
                for c in range(NCH)]
    xs_spill = idram("xs_spill", (DSL, XSW))
    xc_spill = [idram(f"xc_spill{d}", (DSL, T)) for d in range(2)]
    z_spill = idram("z_spill", (DIN, TOWN))
    u1_spill = idram("u1_spill", (DM, TOWN))
    x1_spill = idram("x1_spill", (TOWN, DM))
    yf_spill = idram("yf_spill", (DSL, T), BF16)
    mw_bf = idram("mw_bf", (16 * 128, 32 * 128), BF16)
    xf_spill = idram("xf_spill", (DM, TOWN), BF16)
    gw1_bf = idram("gw1_bf", (DM, 2048), BF16)
    ws1_bf = idram("ws1_bf", (DM, 2048), BF16)
    ws2_bf = idram("ws2_bf", (2048, DM), BF16)
    we1_bf = idram("we1_bf", (8 * DM, 2048), BF16)
    we2_bf = idram("we2_bf", (8 * 2048, DM), BF16)


    dbg_outs = {}
    for nm in debug_names:
        h, shape, dt = internals[nm]
        dbg_outs[nm] = nc.dram_tensor(f"dbg_{nm}", list(shape), dt,
                                      kind="ExternalOutput")

    es = ExitStack()

    def sb(st, name, shape, dt=F32):
        return st.enter_context(nc.sbuf_tensor(name, list(shape), dt))

    def psum(st, name, shape, dt=F32):
        return st.enter_context(nc.psum_tensor(name, list(shape), dt))

    with es:
        sems = {}
        for e in ENGINES:
            sems[(e, "seq")] = es.enter_context(nc.semaphore(f"sq_{e}"))
        for e in ("sync", "gpsimd"):
            for i in range(NDMASEM):
                sems[(e, "dma", i)] = es.enter_context(
                    nc.semaphore(f"dm_{e}{i}"))

        identf = sb(es, "identf", (128, 128))
        identb = sb(es, "identb", (128, 128), BF16)
        onesf = sb(es, "onesf", (1, 128))
        gam = sb(es, "gam", (128, 8))
        bet = sb(es, "bet", (128, 8))
        convw = sb(es, "convw", (128, 16))
        convb = sb(es, "convb", (128, 4))
        dskip = sb(es, "dskip", (128, 4))
        dtbias = sb(es, "dtbias", (128, 4))
        negA = sb(es, "negA", (128, 64))
        gb2s = sb(es, "gb2s", (8, 1))
        xprojw_b = sb(es, "xprojw_b", (128, 2 * 2 * 96), BF16)
        dtw_b = sb(es, "dtw_b", (64, 2 * 2 * 128), BF16)
        wstg = sb(es, "wstg", (128, 2048))
        pwst = [sb(es, f"pwst{i}", (128, 1024)) for i in range(2)]
        pbst = [sb(es, f"pbst{i}", (128, 1024), BF16) for i in range(2)]
        evac = sb(es, "evac", (128, 512))
        evac2 = sb(es, "evac2", (128, 512))
        evacbf = sb(es, "evacbf", (128, 2048), BF16)
        epsc = sb(es, "epsc", (128, 1))
        pidsb = sb(es, "pidsb", (1, 1), mybir.dt.uint32)

        def cast_w(name, src, dst_sb, kt, mcols, kdim=128, dst_off=0,
                   deps=()):
            for k in range(kt):
                dma("sync", [(wstg[0:kdim, 0:mcols],
                              src[k * kdim:(k + 1) * kdim, :])],
                    deps=deps, tag=f"lw_{name}{k}", writes=["wstg"])

                def fc(e, k=k, mcols=mcols, kdim=kdim, dst_sb=dst_sb,
                       dst_off=dst_off):
                    return e.tensor_copy(
                        out=dst_sb[0:kdim, dst_off + k * mcols:
                                   dst_off + (k + 1) * mcols],
                        in_=wstg[0:kdim, 0:mcols])
                comp("vector", fc, tag=f"cw_{name}{k}", reads=["wstg"],
                     writes=[f"w_{name}"])

        dma("sync", [
            (identf[:], ident_f[:, :]), (identb[:], ident_b[:, :]),
            (onesf[:], ones_f[:, :]), (gam[:], gamma_t[:, :]),
            (bet[:], beta_t[:, :]), (gb2s[:], gb2c[:, :]),
        ], tag="ldc", writes=["consts"])
        dma("sync", [
            (convw[:, 0:8], conv_w_t[0][:, :, :].rearrange("p a b -> p (a b)")),
            (convw[:, 8:16], conv_w_t[1][:, :, :].rearrange("p a b -> p (a b)")),
            (convb[:, 0:2], conv_b_t[0][:, :]),
            (convb[:, 2:4], conv_b_t[1][:, :]),
            (dskip[:, 0:2], dskip_t[0][:, :]),
            (dskip[:, 2:4], dskip_t[1][:, :]),
            (dtbias[:, 0:2], dtb_t[0][:, :]),
            (dtbias[:, 2:4], dtb_t[1][:, :]),
            (negA[:, 0:32], alog_t[0][:, :, :].rearrange("p a b -> p (a b)")),
            (negA[:, 32:64], alog_t[1][:, :, :].rearrange("p a b -> p (a b)")),
        ], tag="lds", writes=["smallw"])
        comp("vector", lambda e: e.memset(epsc[:], EPS), tag="mseps",
             writes=["consts"])
        dma("sync", [(pidsb[:], nc.partition_id_tensor.ap()[:, :])],
            tag="lpid", writes=["pidb"])

        comp("scalar", lambda e: e.activation(out=negA[:], in_=negA[:],
                                              func=AF.Exp),
             tag="negA_e", reads=["smallw"], writes=["smallw"])
        comp("vector", lambda e: e.tensor_scalar_mul(out=negA[:], in0=negA[:],
                                                     scalar1=-1.0),
             tag="negA_n", reads=["smallw"], writes=["smallw"])
        for d in range(2):
            cast_w(f"xp{d}", xproj_w[d][:, :], xprojw_b, 2, 96,
                   dst_off=d * 192)
            cast_w(f"dw{d}", dt_w[d][:, :], dtw_b, 1, 256, kdim=64,
                   dst_off=d * 256)
        for mt in range(16):
            for h in range(2):
                dma("sync", [(wstg[:, 0:2048],
                              dap(merge_w_p, mt * 128 * 4096 + h * 2048,
                                  [[4096, 128], [1, 2048]]))],
                    tag=f"lmw{mt}_{h}", writes=["wstg"])
                comp("vector", lambda e: e.tensor_copy(
                    out=evacbf[:, 0:2048], in_=wstg[:, 0:2048]),
                    tag=f"cmw{mt}_{h}", reads=["wstg"], writes=["evbf"])
                dma("sync", [(dap(mw_bf, mt * 128 * 4096 + h * 2048,
                                  [[4096, 128], [1, 2048]]),
                              evacbf[:, 0:2048])],
                    tag=f"smw{mt}_{h}", reads=["evbf"])

        pcjobs = []
        pcjobs.append((gw1, 0, gw1_bf, 0, 8, 2048))
        pcjobs.append((ws1, 0, ws1_bf, 0, 8, 2048))
        pcjobs.append((ws2, 0, ws2_bf, 0, 16, 1024))
        for ee in range(8):
            pcjobs.append((we1, ee * DM * 2048, we1_bf, ee * DM * 2048,
                           8, 2048))
        for ee in range(8):
            pcjobs.append((we2, ee * 2048 * DM, we2_bf, ee * 2048 * DM,
                           16, 1024))
        pcitems = []
        for srch, soff, dsth, doff, kt, mcols in pcjobs:
            for k in range(kt):
                for h in range(mcols // 1024):
                    off = k * 128 * mcols + h * 1024
                    pcitems.append((srch, soff + off, dsth, doff + off,
                                    mcols))
        pcstate = {"i": 0}

        def pc_drip(nn):
            i0 = pcstate["i"]
            for t in range(i0, min(i0 + nn, len(pcitems))):
                srch, soff, dsth, doff, mcols = pcitems[t]
                par = t % 2
                dma("gpsimd", [(pwst[par][:, :],
                                dap(srch, soff,
                                    [[mcols, 128], [1, 1024]]))],
                    tag=f"pcl{t}", writes=[f"pcw{par}"])
                comp("scalar", lambda e, par=par: e.copy(
                    out=pbst[par][:, :], in_=pwst[par][:, :]),
                    tag=f"pcc{t}", reads=[f"pcw{par}"],
                    writes=[f"pcb{par}"])
                dma("gpsimd", [(dap(dsth, doff,
                                    [[mcols, 128], [1, 1024]]),
                                pbst[par][:, :])],
                    tag=f"pcs{t}", reads=[f"pcb{par}"])
            pcstate["i"] = min(i0 + nn, len(pcitems))

        st1 = ExitStack()
        with st1:
            inw_xs_b = sb(st1, "inw_xs_b", (128, 8 * 256), BF16)
            inw_z_b = sb(st1, "inw_z_b", (128, 8 * 2048), BF16)
            xtm = [sb(st1, f"xtm{i}", (128, DM)) for i in range(2)]
            u1tm = [sb(st1, f"u1tm{i}", (128, DM)) for i in range(2)]
            u2tm = [sb(st1, f"u2tm{i}", (128, DM)) for i in range(2)]
            sq = [sb(st1, f"sq{i}", (128, DM)) for i in range(2)]
            stat = [sb(st1, f"stat{i}", (128, 8)) for i in range(2)]
            h2sb = sb(st1, "h2sb", (128, 8 * TOWN), BF16)
            u1fm = [sb(st1, f"u1fm{i}", (128, 8 * BLK)) for i in range(2)]
            h2rd = sb(st1, "h2rd", (128, 2 * 8 * 512), BF16)
            p1_mm = [psum(st1, f"p1m{i}", (128, 512)) for i in range(4)]
            p1_tr = [psum(st1, f"p1t{i}", (128, 128)) for i in range(2)]

            cast_w("inwxs", in_w_xs[:, :], inw_xs_b, 8, 256)
            cast_w("inwz", in_w_z[:, :], inw_z_b, 8, 2048)

            def ln_stats(src, dst, rbuf, wbuf, tag, pp):
                stp, sqp_ = stat[pp], sq[pp]

                def f_a(e, src=src, dst=dst):
                    e.reduce_sum(out=stp[:, 0:1], in_=src, axis=AX.X)
                    e.drain()
                    e.tensor_scalar_mul(out=stp[:, 0:1],
                                        in0=stp[:, 0:1], scalar1=RD)
                    e.drain()
                    return e.tensor_scalar(out=dst, in0=src,
                                           scalar1=stp[:, 0:1],
                                           scalar2=None, op0=ALU.subtract)
                comp("vector", f_a, tag=f"{tag}a", reads=[rbuf],
                     writes=[wbuf, f"stat{pp}"])
                comp("scalar", lambda e, dst=dst: e.activation(
                    out=sqp_[:], in_=dst, func=AF.Square),
                    tag=f"{tag}s", reads=[wbuf], writes=[f"sq{pp}"])

                def f_v(e):
                    e.reduce_sum(out=stp[:, 1:2], in_=sqp_[:], axis=AX.X)
                    e.drain()
                    return e.tensor_scalar_mul(out=stp[:, 1:2],
                                               in0=stp[:, 1:2], scalar1=RD)
                comp("vector", f_v, tag=f"{tag}v", reads=[f"sq{pp}"],
                     writes=[f"stat{pp}"])

                def f_r(e):
                    e.activation(out=stp[:, 2:3], in_=stp[:, 1:2],
                                 func=AF.Ln, bias=epsc[:, 0:1])
                    return e.activation(out=stp[:, 3:4], in_=stp[:, 2:3],
                                        func=AF.Exp, scale=-0.5)
                comp("scalar", f_r, tag=f"{tag}r", reads=[f"stat{pp}"],
                     writes=[f"stat{pp}"])
                comp("vector", lambda e, dst=dst: e.tensor_scalar_mul(
                    out=dst, in0=dst, scalar1=stp[:, 3:4]),
                    tag=f"{tag}u", reads=[wbuf, f"stat{pp}"], writes=[wbuf])

            for b in range(NBLK):
                bp = b % 2
                dma("sync", [(xtm[bp][:], x_own[b * BLK:(b + 1) * BLK, :])],
                    tag=f"ldx{b}", writes=[f"B_x{bp}"])
                ln_stats(xtm[bp][:], u1tm[bp][:], f"B_x{bp}", f"B_u1{bp}",
                         f"l1_{b}", bp)
                ln_stats(u1tm[bp][:], u2tm[bp][:], f"B_u1{bp}", f"B_u2{bp}",
                         f"l2_{b}", bp)
                for ft in range(8):
                    def f_t1(e, ft=ft, bp=bp):
                        return e.transpose(
                            p1_tr[0][:],
                            u1tm[bp][:, ft * 128:(ft + 1) * 128],
                            identf[:])
                    comp("tensor", f_t1, tag=f"t1_{b}_{ft}",
                         reads=[f"B_u1{bp}", "consts"], writes=["p_tr0"])

                    def f_e1(e, ft=ft, bp=bp):
                        return e.copy(
                            out=u1fm[bp][:, ft * BLK:(ft + 1) * BLK],
                            in_=p1_tr[0][:])
                    comp("scalar", f_e1, tag=f"e1_{b}_{ft}",
                         reads=["p_tr0"], writes=[f"u1fm{bp}"])

                    def f_t2(e, ft=ft, bp=bp):
                        return e.transpose(
                            p1_tr[1][:],
                            u2tm[bp][:, ft * 128:(ft + 1) * 128],
                            identf[:])
                    comp("tensor", f_t2, tag=f"t2_{b}_{ft}",
                         reads=[f"B_u2{bp}", "consts"], writes=["p_tr1"])

                    def f_e2(e, ft=ft, b=b):
                        return e.tensor_scalar(
                            out=h2sb[:, ft * TOWN + b * BLK:
                                     ft * TOWN + (b + 1) * BLK],
                            in0=p1_tr[1][:], scalar1=gam[:, ft:ft + 1],
                            scalar2=bet[:, ft:ft + 1], op0=ALU.mult,
                            op1=ALU.add)
                    comp("vector", f_e2, tag=f"e2_{b}_{ft}",
                         reads=["p_tr1", "consts"], writes=["h2sb"])

                dma("sync", [(dap(u1_spill, b * BLK,
                                  [[TOWN, 128], [128 * TOWN, 8], [1, BLK]]),
                    u1fm[bp][:].rearrange("p (a t) -> p a t", a=8))],
                    tag=f"su1{b}", reads=[f"u1fm{bp}"])
                pc_drip(2)
                if b % 4 == 3:
                    tt = b // 4
                    for m in range(16):
                        def f_z(e, m=m, tt=tt):
                            last = None
                            for k in range(8):
                                last = e.matmul(
                                    p1_mm[m % 2][:],
                                    inw_z_b[:, k * 2048 + m * 128:
                                            k * 2048 + (m + 1) * 128],
                                    h2sb[:, k * TOWN + tt * 512:
                                         k * TOWN + (tt + 1) * 512],
                                    start=(k == 0), stop=(k == 7))
                            return last
                        comp("tensor", f_z, tag=f"z{m}_{tt}",
                             reads=["h2sb", "w_inwz"], writes=[f"p1m{m % 2}"])
                        comp("scalar", lambda e, m=m: e.copy(
                            out=evac[:], in_=p1_mm[m % 2][:]),
                            tag=f"ze{m}_{tt}", reads=[f"p1m{m % 2}"],
                            writes=["evac"])
                        dma("sync", [(z_spill[m * 128:(m + 1) * 128,
                                              tt * 512:(tt + 1) * 512],
                                      evac[:])],
                            tag=f"zs{m}_{tt}", reads=["evac"])

            dma("sync", [(dap(ag_h2_in, 0,
                              [[TOWN, 128], [128 * TOWN, 8], [1, TOWN]]),
                          h2sb[:].rearrange("p (a t) -> p a t", a=8))],
                tag="sh2", reads=["h2sb"])
            ag_h2 = comp("gpsimd", lambda e: e.collective_compute(
                "AllGather", ALU.bypass, replica_groups=[list(range(NCORES))],
                ins=[ag_h2_in.ap().opt()], outs=[ag_h2_out.ap().opt()]),
                deps=[sc.steps[-1]], tag="ag_h2", coll=True)

            for r in range(NCORES):
                for lc in range(4):
                    pb = (r * 4 + lc) % 2
                    dma("sync", [(h2rd[:, pb * 4096:(pb + 1) * 4096].rearrange(
                        "p (a t) -> p a t", a=8),
                        dap(ag_h2_out, r * DM * TOWN + lc * 512,
                            [[TOWN, 128], [128 * TOWN, 8], [1, 512]]))],
                        deps=[ag_h2], tag=f"lh2_{r}_{lc}",
                        writes=[f"h2rd{pb}"])
                    for m in range(2):
                        def f_xs(e, lc=lc, m=m, pb=pb):
                            last = None
                            for k in range(8):
                                last = e.matmul(
                                    p1_mm[2 + m][:],
                                    inw_xs_b[:, k * 256 + m * 128:
                                             k * 256 + (m + 1) * 128],
                                    h2rd[:, pb * 4096 + k * 512:
                                         pb * 4096 + (k + 1) * 512],
                                    start=(k == 0), stop=(k == 7))
                            return last
                        comp("tensor", f_xs, tag=f"xs{r}_{lc}_{m}",
                             reads=[f"h2rd{pb}", "w_inwxs"],
                             writes=[f"p1m{2 + m}"])
                        ev = evac if m == 0 else evac2
                        comp("scalar", lambda e, m=m, ev=ev: e.copy(
                            out=ev[:], in_=p1_mm[2 + m][:]),
                            tag=f"xe{r}_{lc}_{m}", reads=[f"p1m{2 + m}"],
                            writes=[f"exv{m}"])
                        dst = dap(xs_spill,
                                  m * 128 * XSW + 4 + r * TOWN + lc * 512,
                                  [[XSW, 128], [1, 512]])
                        dma("sync", [(dst, ev[:])], tag=f"xss{r}_{lc}_{m}",
                            reads=[f"exv{m}"])
                    pc_drip(5)

            comp("vector", lambda e: e.memset(stat[0][:], 0.0), tag="zp",
                 writes=["stat0"])
            dma("sync", [
                (dap(xs_spill, 0, [[XSW, 128], [1, 4]]), stat[0][0:128, 0:4]),
                (dap(xs_spill, 128 * XSW, [[XSW, 128], [1, 4]]),
                 stat[0][0:128, 4:8]),
                (dap(xs_spill, T + 4, [[XSW, 128], [1, 4]]),
                 stat[0][0:128, 0:4]),
                (dap(xs_spill, 128 * XSW + T + 4, [[XSW, 128], [1, 4]]),
                 stat[0][0:128, 4:8]),
            ], tag="zps", reads=["stat0"])

        sc.barrier()

        st2 = ExitStack()
        with st2:
            ind32b = sb(st2, "ind32b_bf", (32, 32 * 128), BF16)
            dtf32 = sb(st2, "dtf32", (64, CH))
            dtbf = sb(st2, "dtbf", (64, CH), BF16)
            bcrows = sb(st2, "bcrows", (32, CH))
            bcbf = sb(st2, "bcbf", (32, CH), BF16)
            CW = CCH + 8
            xschs = [sb(st2, f"xsch{i}", (128, 2 * CW)) for i in range(2)]
            xcch = sb(st2, "xcch", (128, 2 * CCH))
            xcchs = [xcch, xcch]
            xcbf = sb(st2, "xcbf", (128, 2 * CCH), BF16)
            dtsb = sb(st2, "dtsb", (128, 2 * CH))
            csb = sb(st2, "csb", (128, 2 * CH), BF16)
            spsb = sb(st2, "spsb", (128, 2 * CH))
            vsb = sb(st2, "vsb", (128, 2 * CH))
            xcsc = sb(st2, "xcsc", (128, 2 * CH))
            dAsb = sb(st2, "dAsb", (128, 4 * CH))
            dBsb = sb(st2, "dBsb", (128, 4 * CH))
            hsb = sb(st2, "hsb", (128, 4 * CH))
            gsb = sb(st2, "gsb", (128, 4 * CH), BF16)
            ysb = sb(st2, "ysb", (128, 2 * CH), BF16)
            statev = sb(st2, "statev", (128, 2 * NST))
            p_bc = [psum(st2, f"p_bc{i}", (128, CH)) for i in range(2)]
            p_y = [psum(st2, f"p_y{i}", (128, CH)) for i in range(2)]

            for hh in range(2):
                dma("sync", [(wstg[0:32, 0:2048],
                              ind32_f[:, hh * 2048:(hh + 1) * 2048])],
                    tag=f"lind{hh}", writes=["wstg"])
                comp("vector", lambda e, hh=hh: e.tensor_copy(
                    out=ind32b[:, hh * 2048:(hh + 1) * 2048],
                    in_=wstg[0:32, 0:2048]),
                    tag=f"cind{hh}", reads=["wstg"], writes=["ind32bb"])

            arsteps = {}
            for d in range(2):
                xps_steps = []
                for cc in range(T // CCH):
                    par = (d * 8 + cc) % 2
                    xs_c = xschs[par]
                    dma("sync", [
                        (xs_c[:, 0:CW],
                         dap(xs_spill, cc * CCH, [[XSW, 128], [1, CW]])),
                        (xs_c[:, CW:2 * CW],
                         dap(xs_spill, 128 * XSW + cc * CCH,
                             [[XSW, 128], [1, CW]])),
                    ], tag=f"lxs{d}_{cc}", writes=[f"xsch{par}"])

                    xc_c = xcchs[par]

                    def f_conv(e, d=d, xs_c=xs_c, xc_c=xc_c):
                        last = None
                        for k in range(4):
                            if k > 0:
                                e.drain()
                            for tl in range(2):
                                base = tl * CW
                                off = (1 + k) if d == 0 else (7 - k)
                                wcol = convw[:, d * 8 + tl * 4 + k:
                                             d * 8 + tl * 4 + k + 1]
                                win = xs_c[:, base + off:base + off + CCH]
                                o = xc_c[:, tl * CCH:(tl + 1) * CCH]
                                if k == 0:
                                    last = e.tensor_scalar_mul(
                                        out=o, in0=win, scalar1=wcol)
                                else:
                                    last = e.scalar_tensor_tensor(
                                        out=o, in0=win, scalar=wcol, in1=o,
                                        op0=ALU.mult, op1=ALU.add)
                        return last
                    comp("vector", f_conv, tag=f"cv{d}_{cc}",
                         reads=[f"xsch{par}", "smallw"], writes=["xcch"])

                    def f_silu(e, d=d, xc_c=xc_c):
                        last = None
                        for tl in range(2):
                            last = e.activation(
                                out=xc_c[:, tl * CCH:(tl + 1) * CCH],
                                in_=xc_c[:, tl * CCH:(tl + 1) * CCH],
                                func=AF.Silu,
                                bias=convb[:, d * 2 + tl:d * 2 + tl + 1])
                        return last
                    comp("scalar", f_silu, tag=f"si{d}_{cc}",
                         reads=["xcch", "smallw"], writes=["xcch"])
                    dma("sync", [(xc_spill[d][0:128, cc * CCH:(cc + 1) * CCH],
                                  xc_c[:, 0:CCH]),
                                 (xc_spill[d][128:256, cc * CCH:(cc + 1) * CCH],
                                  xc_c[:, CCH:2 * CCH])],
                        tag=f"sxc{d}_{cc}", reads=["xcch"])
                    comp("vector", lambda e, xc_c=xc_c: e.tensor_copy(
                        out=xcbf[:], in_=xc_c[:]),
                        tag=f"cb{d}_{cc}", reads=["xcch"],
                        writes=["xcbf"])
                    pc_drip(5)

                    for nt in range(CCH // 512):
                        def f_xp(e, d=d, nt=nt):
                            last = None
                            for tl in range(2):
                                last = e.matmul(
                                    p_bc[nt % 2][0:96, 0:512],
                                    xprojw_b[:, (d * 2 + tl) * 96:
                                             (d * 2 + tl + 1) * 96],
                                    xcbf[:, tl * CCH + nt * 512:
                                         tl * CCH + (nt + 1) * 512],
                                    start=(tl == 0), stop=(tl == 1))
                            return last
                        comp("tensor", f_xp, tag=f"xp{d}_{cc}_{nt}",
                             reads=["xcbf", "w_xp0", "w_xp1"],
                             writes=[f"p_bc{nt % 2}"])
                        comp("scalar", lambda e, nt=nt: e.copy(
                            out=evac[0:96, :], in_=p_bc[nt % 2][0:96, 0:512]),
                            tag=f"xpe{d}_{cc}_{nt}", reads=[f"p_bc{nt % 2}"],
                            writes=["evac"])
                        xps_steps.append(
                            dma("sync", [(ar_in[d][:, cc * CCH + nt * 512:
                                                   cc * CCH + (nt + 1) * 512],
                                          evac[0:96, :])],
                                tag=f"xps{d}_{cc}_{nt}", reads=["evac"]))

                arsteps[d] = comp(
                    "gpsimd", lambda e, d=d: e.collective_compute(
                        "AllReduce", ALU.add,
                        replica_groups=[list(range(NCORES))],
                        ins=[ar_in[d].ap().opt()], outs=[ar_out[d].ap().opt()]),
                    deps=[sc.steps[-1]] + xps_steps, tag=f"ar{d}", coll=True)

            for d in range(2):
                arstep = arsteps[d]
                chunks = list(range(NCH)) if d == 0 else \
                    list(range(NCH - 1, -1, -1))
                for ci, c in enumerate(chunks):
                    first = (ci == 0)
                    c0 = c * CH
                    dma("sync", [(dtf32[:, :], ar_out[d][0:64, c0:c0 + CH]),
                                 (bcrows[:, :], ar_out[d][64:96, c0:c0 + CH])],
                        deps=[arstep], tag=f"lxd{d}_{c}",
                        writes=["dtf32", "bcrows"])
                    def f_cxd(e):
                        e.tensor_copy(out=dtbf[:, :], in_=dtf32[:, :])
                        return e.tensor_copy(out=bcbf[:, :], in_=bcrows[:, :])
                    comp("vector", f_cxd, tag=f"cxd{d}_{c}",
                         reads=["dtf32", "bcrows"], writes=["dtbf", "bcbf"])
                    for tl in range(2):
                        for nt in range(2):
                            def f_dtp(e, d=d, tl=tl, nt=nt):
                                return e.matmul(
                                    p_y[tl][:, nt * 512:(nt + 1) * 512],
                                    dtw_b[0:64, (d * 2 + tl) * 128:
                                          (d * 2 + tl + 1) * 128],
                                    dtbf[0:64, nt * 512:(nt + 1) * 512],
                                    start=True, stop=True)
                            comp("tensor", f_dtp, tag=f"dtp{d}_{c}_{tl}_{nt}",
                                 reads=["dtbf", "w_dw0", "w_dw1"],
                                 writes=[f"p_y{tl}"])

                    def f_sp_s(e, d=d):
                        last = None
                        for tl in range(2):
                            bcol = dtbias[:, d * 2 + tl:d * 2 + tl + 1]
                            sl = spsb[:, tl * CH:(tl + 1) * CH]
                            e.activation(out=sl, in_=p_y[tl][:], func=AF.Abs,
                                         bias=bcol)
                            e.activation(out=sl, in_=sl, func=AF.Exp,
                                         scale=-1.0)
                            last = e.activation(out=sl, in_=sl, func=AF.Ln,
                                                bias=1.0)
                        return last
                    comp("scalar", f_sp_s, tag=f"sp{d}_{c}",
                         reads=["p_y0", "p_y1", "smallw"], writes=["spsb"])

                    def f_sp_v(e, d=d):
                        for tl in range(2):
                            bcol = dtbias[:, d * 2 + tl:d * 2 + tl + 1]
                            e.tensor_scalar(out=dtsb[:, tl * CH:(tl + 1) * CH],
                                            in0=p_y[tl][:], scalar1=bcol,
                                            scalar2=0.0, op0=ALU.add,
                                            op1=ALU.max)
                        e.drain()
                        last = None
                        for tl in range(2):
                            last = e.tensor_add(
                                out=dtsb[:, tl * CH:(tl + 1) * CH],
                                in0=dtsb[:, tl * CH:(tl + 1) * CH],
                                in1=spsb[:, tl * CH:(tl + 1) * CH])
                        return last
                    comp("vector", f_sp_v, tag=f"spv{d}_{c}",
                         reads=["p_y0", "p_y1", "spsb", "smallw"],
                         writes=["dtsb"])

                    dma("sync", [(xcsc[:, 0:CH],
                                  xc_spill[d][0:128, c0:c0 + CH]),
                                 (xcsc[:, CH:2 * CH],
                                  xc_spill[d][128:256, c0:c0 + CH])],
                        tag=f"lxc{d}_{c}", writes=["xcsc"])
                    comp("vector", lambda e: e.tensor_mul(
                        out=vsb[:], in0=dtsb[:], in1=xcsc[:]),
                        tag=f"v{d}_{c}", reads=["dtsb", "xcsc"],
                        writes=["vsb"])

                    def mk_bb(n):
                        def f_bb(e, n=n):
                            last = None
                            for half in range(2):
                                last = e.matmul(
                                    p_bc[0][:, half * 512:(half + 1) * 512],
                                    ind32b[:, n * 128:(n + 1) * 128],
                                    bcbf[:, half * 512:(half + 1) * 512],
                                    start=True, stop=True)
                            return last
                        comp("tensor", f_bb, tag=f"bb{d}_{c}_{n}",
                             reads=["bcbf", "ind32bb"], writes=["p_bc0"])

                        def f_cb2(e, n=n):
                            last = None
                            for half in range(2):
                                last = e.matmul(
                                    p_bc[1][:, half * 512:(half + 1) * 512],
                                    ind32b[:, (16 + n) * 128:(17 + n) * 128],
                                    bcbf[:, half * 512:(half + 1) * 512],
                                    start=True, stop=True)
                            return last
                        comp("tensor", f_cb2, tag=f"cb{d}_{c}_{n}",
                             reads=["bcbf", "ind32bb"], writes=["p_bc1"])
                        comp("scalar", lambda e, n=n: e.copy(
                            out=csb[:, (n % 2) * CH:(n % 2 + 1) * CH],
                            in_=p_bc[1][:]),
                            tag=f"ec{d}_{c}_{n}", reads=["p_bc1"],
                            writes=[f"bC{n % 2}"])

                    mk_bb(0)
                    for n in range(NST):
                        nb = n % 2
                        slots = [nb * 2 + tl for tl in range(2)]
                        dAs = [dAsb[:, s * CH:(s + 1) * CH] for s in slots]
                        dBs = [dBsb[:, s * CH:(s + 1) * CH] for s in slots]
                        hhs = [hsb[:, s * CH:(s + 1) * CH] for s in slots]
                        ggs = [gsb[:, s * CH:(s + 1) * CH] for s in slots]

                        for tl in range(2):
                            def f_dA(e, d=d, tl=tl, n=n, dA=dAs[tl]):
                                return e.activation(
                                    out=dA,
                                    in_=dtsb[:, tl * CH:(tl + 1) * CH],
                                    func=AF.Exp,
                                    scale=negA[:, d * 32 + tl * 16 + n:
                                               d * 32 + tl * 16 + n + 1])
                            comp("scalar", f_dA, tag=f"dA{d}_{c}_{n}_{tl}",
                                 reads=["dtsb", "smallw"],
                                 writes=[f"bA{slots[tl]}"])

                        for tl in range(2):
                            def f_dB(e, tl=tl, dB=dBs[tl]):
                                return e.tensor_mul(
                                    out=dB,
                                    in0=vsb[:, tl * CH:(tl + 1) * CH],
                                    in1=p_bc[0][:])
                            comp("vector", f_dB, tag=f"dB{d}_{c}_{n}_{tl}",
                                 reads=["vsb", "p_bc0"],
                                 writes=[f"bB{slots[tl]}"])

                        if n + 1 < NST:
                            mk_bb(n + 1)

                        for tl in range(2):
                            def f_scan(e, d=d, tl=tl, n=n, dA=dAs[tl],
                                       dB=dBs[tl], hh=hhs[tl], first=first):
                                scol = statev[:, tl * NST + n:
                                              tl * NST + n + 1]
                                init = 0.0 if first else scol
                                if d == 0:
                                    return e.tensor_tensor_scan(
                                        out=hh, data0=dA, data1=dB,
                                        initial=init, op0=ALU.mult,
                                        op1=ALU.add)
                                return e.tensor_tensor_scan(
                                    out=hh[:, ::-1], data0=dA[:, ::-1],
                                    data1=dB[:, ::-1], initial=init,
                                    op0=ALU.mult, op1=ALU.add)
                            comp("vector", f_scan, tag=f"sn{d}_{c}_{n}_{tl}",
                                 reads=[f"bA{slots[tl]}", f"bB{slots[tl]}",
                                        "statev"],
                                 writes=[f"bh{slots[tl]}"])

                        for tl in range(2):
                            def f_st(e, d=d, tl=tl, n=n, hh=hhs[tl]):
                                scol = statev[:, tl * NST + n:
                                              tl * NST + n + 1]
                                src = hh[:, CH - 1:CH] if d == 0 \
                                    else hh[:, 0:1]
                                return e.copy(out=scol, in_=src)
                            comp("scalar", f_st, tag=f"sv{d}_{c}_{n}_{tl}",
                                 reads=[f"bh{slots[tl]}"], writes=["statev"])

                        for tl in range(2):
                            def f_g(e, gg=ggs[tl], hh=hhs[tl], nb=nb):
                                return e.tensor_mul(
                                    out=gg, in0=hh,
                                    in1=csb[:, nb * CH:(nb + 1) * CH])
                            comp("vector", f_g, tag=f"g{d}_{c}_{n}_{tl}",
                                 reads=[f"bh{slots[tl]}", f"bC{nb}"],
                                 writes=[f"bg{slots[tl]}"])

                        for tl in range(2):
                            def f_red(e, tl=tl, n=n, gg=ggs[tl]):
                                last = None
                                for half in range(2):
                                    last = e.matmul(
                                        p_y[tl][:, half * 512:
                                                (half + 1) * 512],
                                        identb[:],
                                        gg[:, half * 512:(half + 1) * 512],
                                        start=(n == 0), stop=(n == NST - 1))
                                return last
                            rw = [f"p_y{tl}"] if (n == 0 or n == NST - 1) \
                                else ()
                            comp("tensor", f_red, tag=f"rd{d}_{c}_{n}_{tl}",
                                 reads=[f"bg{slots[tl]}", "consts"],
                                 writes=rw)

                    def f_y(e, d=d):
                        last = None
                        for tl in range(2):
                            last = e.scalar_tensor_tensor(
                                out=ysb[:, tl * CH:(tl + 1) * CH],
                                in0=xcsc[:, tl * CH:(tl + 1) * CH],
                                scalar=dskip[:, d * 2 + tl:d * 2 + tl + 1],
                                in1=p_y[tl][:], op0=ALU.mult, op1=ALU.add)
                        return last
                    comp("vector", f_y, tag=f"y{d}_{c}",
                         reads=["xcsc", "p_y0", "p_y1", "smallw"],
                         writes=["ysb"])

                    pc_drip(12)
                    if d == 0:
                        dma("sync", [(yf_spill[0:128, c0:c0 + CH],
                                      ysb[:, 0:CH]),
                                     (yf_spill[128:256, c0:c0 + CH],
                                      ysb[:, CH:2 * CH])],
                            tag=f"sy{c}", reads=["ysb"])
                    else:
                        dma("sync", [(ag_y_in[c][256:384, :], ysb[:, 0:CH]),
                                     (ag_y_in[c][384:512, :],
                                      ysb[:, CH:2 * CH]),
                                     (ag_y_in[c][0:256, :],
                                      yf_spill[:, c0:c0 + CH])],
                            tag=f"syb{c}", reads=["ysb"])
                        comp("gpsimd", lambda e, c=c:
                             e.collective_compute(
                                 "AllGather", ALU.bypass,
                                 replica_groups=[list(range(NCORES))],
                                 ins=[ag_y_in[c].ap().opt()],
                                 outs=[ag_y_out[c].ap().opt()]),
                             deps=[sc.steps[-1]], tag=f"agy{c}",
                             coll=True)
            pc_drip(len(pcitems))
            sc.barrier()

        agy = {int(s.tag[3:]): s for s in sc.steps if s.tag.startswith("agy")}
        st3 = ExitStack()
        with st3:
            ycsb = [sb(st3, f"ycsb{i}", (128, 32 * 512), BF16)
                    for i in range(2)]
            mwb = [sb(st3, f"mwb{i}", (128, 32 * 128), BF16)
                   for i in range(2)]
            gosb = sb(st3, "gosb", (128, 16 * 512), BF16)
            zsb = sb(st3, "zsb", (128, 512))
            sizsb = sb(st3, "sizsb", (128, 512))
            outw_b = sb(st3, "outw_b", (128, 16 * 1024), BF16)
            u1rd = sb(st3, "u1rd", (128, 512))
            asb = sb(st3, "asb", (128, 8 * 512))
            atm = sb(st3, "atm", (128, DM))
            sqp = sb(st3, "sqp", (128, DM))
            statp = sb(st3, "statp", (128, 8))
            x1tm = sb(st3, "x1tm", (128, DM))
            xrld = sb(st3, "xrld", (128, DM))
            u2p = sb(st3, "u2p", (128, DM))
            xfbt = sb(st3, "xfbt", (128, 8 * BLK), BF16)
            p4_mm = [psum(st3, f"p4m{i}", (128, 512)) for i in range(4)]
            p4_tr = psum(st3, "p4t", (128, 128))

            cast_w("outw", out_w[:, :], outw_b, 16, 1024)

            def ln_blk(src_sb, dst_sb, mu_c, var_c, tag, rbuf, wbuf):
                def f_a(e):
                    e.reduce_sum(out=statp[:, mu_c:mu_c + 1], in_=src_sb,
                                 axis=AX.X)
                    e.drain()
                    e.tensor_scalar_mul(out=statp[:, mu_c:mu_c + 1],
                                        in0=statp[:, mu_c:mu_c + 1],
                                        scalar1=RD)
                    e.drain()
                    return e.tensor_scalar(out=dst_sb, in0=src_sb,
                                           scalar1=statp[:, mu_c:mu_c + 1],
                                           scalar2=None, op0=ALU.subtract)
                comp("vector", f_a, tag=f"{tag}a", reads=[rbuf],
                     writes=[wbuf, "statp"])
                comp("scalar", lambda e: e.activation(out=sqp[:], in_=dst_sb,
                                                      func=AF.Square),
                     tag=f"{tag}s", reads=[wbuf], writes=["sqp"])

                def f_v(e):
                    e.reduce_sum(out=statp[:, var_c:var_c + 1], in_=sqp[:],
                                 axis=AX.X)
                    e.drain()
                    return e.tensor_scalar_mul(
                        out=statp[:, var_c:var_c + 1],
                        in0=statp[:, var_c:var_c + 1], scalar1=RD)
                comp("vector", f_v, tag=f"{tag}v", reads=["sqp"],
                     writes=["statp"])

                def f_r(e):
                    e.activation(out=statp[:, var_c + 1:var_c + 2],
                                 in_=statp[:, var_c:var_c + 1], func=AF.Ln,
                                 bias=epsc[:, 0:1])
                    return e.activation(out=statp[:, var_c + 2:var_c + 3],
                                        in_=statp[:, var_c + 1:var_c + 2],
                                        func=AF.Exp, scale=-0.5)
                comp("scalar", f_r, tag=f"{tag}r", reads=["statp"],
                     writes=["statp"])
                comp("vector", lambda e: e.tensor_scalar_mul(
                    out=dst_sb, in0=dst_sb,
                    scalar1=statp[:, var_c + 2:var_c + 3]),
                    tag=f"{tag}u", reads=[wbuf, "statp"], writes=[wbuf])

            for tt in range(4):
                tp_ = tt % 2
                lysem = {}

                def f_lyc(e, tt=tt, tp_=tp_, lysem=lysem):
                    last = None
                    for arm in e.switch_core_id(n=128):
                        j = arm.logical % 8
                        c = 2 * j + tt // 2
                        last = e.dma_start(
                            out=ycsb[tp_][:].rearrange("p (a t) -> p a t",
                                                       a=32),
                            in_=dap(ag_y_out[c], (tt % 2) * 512,
                                    [[1024, 128], [128 * 1024, 32],
                                     [1, 512]]),
                        ).then_inc(lysem["s"], 16)
                    return last
                s = sc._mk("sync", f_lyc,
                           [agy[c] for c in range(NCH)] if tt == 0 else [],
                           1, False, f"lyc{tt}", (), [f"ycsb{tp_}"])
                s.semref = lysem

                for m in range(16):
                    mp = m % 2
                    dma("sync", [(mwb[mp][:],
                                  dap(mw_bf, m * 128 * 4096,
                                      [[4096, 128], [1, 4096]]))],
                        tag=f"lmwb{tt}_{m}", writes=[f"mwb{mp}"])

                    def f_mg(e, m=m, mp=mp, tp_=tp_):
                        last = None
                        for k in range(32):
                            last = e.matmul(
                                p4_mm[m % 2][:],
                                mwb[mp][:, k * 128:(k + 1) * 128],
                                ycsb[tp_][:, k * 512:(k + 1) * 512],
                                start=(k == 0), stop=(k == 31))
                        return last
                    comp("tensor", f_mg, tag=f"mg{tt}_{m}",
                         reads=[f"mwb{mp}", f"ycsb{tp_}"],
                         writes=[f"p4m{m % 2}"])
                    dma("sync", [(zsb[:],
                                  z_spill[m * 128:(m + 1) * 128,
                                          tt * 512:(tt + 1) * 512])],
                        tag=f"lz{tt}_{m}", writes=["zsb"])
                    comp("scalar", lambda e: e.activation(
                        out=sizsb[:], in_=zsb[:], func=AF.Silu),
                        tag=f"sz{tt}_{m}", reads=["zsb"], writes=["sizsb"])
                    comp("vector", lambda e, m=m: e.tensor_mul(
                        out=gosb[:, m * 512:(m + 1) * 512], in0=sizsb[:],
                        in1=p4_mm[m % 2][:]),
                        tag=f"gt{tt}_{m}", reads=["sizsb", f"p4m{m % 2}"],
                        writes=["gosb"])

                for fo in range(8):
                    def f_ow(e, fo=fo):
                        last = None
                        for k in range(16):
                            last = e.matmul(
                                p4_mm[2 + fo % 2][:],
                                outw_b[:, k * 1024 + fo * 128:
                                       k * 1024 + (fo + 1) * 128],
                                gosb[:, k * 512:(k + 1) * 512],
                                start=(k == 0), stop=(k == 15))
                        return last
                    comp("tensor", f_ow, tag=f"ow{tt}_{fo}",
                         reads=["gosb", "w_outw"], writes=[f"p4m{2 + fo % 2}"])
                    dma("sync", [(u1rd[:],
                                  u1_spill[fo * 128:(fo + 1) * 128,
                                           tt * 512:(tt + 1) * 512])],
                        tag=f"lu1{tt}_{fo}", writes=["u1rd"])
                    comp("vector", lambda e, fo=fo: e.tensor_add(
                        out=asb[:, fo * 512:(fo + 1) * 512],
                        in0=u1rd[:],
                        in1=p4_mm[2 + fo % 2][:]),
                        tag=f"aa{tt}_{fo}",
                        reads=["u1rd", f"p4m{2 + fo % 2}"], writes=["asb"])

                for sb_ in range(4):
                    b = tt * 4 + sb_
                    for ft in range(8):
                        def f_ta(e, sb_=sb_, ft=ft):
                            return e.transpose(
                                p4_tr[:],
                                asb[:, ft * 512 + sb_ * 128:
                                    ft * 512 + (sb_ + 1) * 128], identf[:])
                        comp("tensor", f_ta, tag=f"ta{b}_{ft}",
                             reads=["asb", "consts"], writes=["p4t"])
                        comp("scalar", lambda e, ft=ft: e.copy(
                            out=atm[:, ft * 128:(ft + 1) * 128], in_=p4_tr[:]),
                            tag=f"ea{b}_{ft}", reads=["p4t"], writes=["atm"])

                    dma("sync", [(xrld[:], x_own[b * BLK:(b + 1) * BLK, :])],
                        tag=f"lx2{b}", writes=["xrld"])
                    ln_blk(atm[:], x1tm[:], 0, 1, f"l3_{b}", "atm", "x1tm")

                    def f_x1(e):
                        return e.tensor_add(out=x1tm[:], in0=x1tm[:],
                                            in1=xrld[:])
                    comp("vector", f_x1, tag=f"x1{b}",
                         reads=["x1tm", "xrld"], writes=["x1tm"])
                    dma("sync", [(x1_spill[b * BLK:(b + 1) * BLK, :],
                                  x1tm[:])], tag=f"sx1{b}", reads=["x1tm"])

                    ln_blk(x1tm[:], u2p[:], 4, 5, f"l4_{b}", "x1tm", "u2p")
                    for ft in range(8):
                        def f_tu2(e, ft=ft):
                            return e.transpose(
                                p4_tr[:], u2p[:, ft * 128:(ft + 1) * 128],
                                identf[:])
                        comp("tensor", f_tu2, tag=f"tu2{b}_{ft}",
                             reads=["u2p", "consts"], writes=["p4t"])

                        def f_exf(e, ft=ft):
                            return e.copy(out=xfbt[:, ft * BLK:(ft + 1) * BLK],
                                          in_=p4_tr[:])
                        comp("scalar", f_exf, tag=f"exf{b}_{ft}",
                             reads=["p4t"], writes=["xfbt"])
                    dma("sync", [(dap(xf_spill, b * BLK,
                                      [[TOWN, 128], [128 * TOWN, 8],
                                       [1, BLK]]),
                                  xfbt[:].rearrange("p (a t) -> p a t", a=8))],
                        tag=f"sxf{b}", reads=["xfbt"])

        sc.barrier()

        st4 = ExitStack()
        with st4:
            w1t = sb(st4, "w1t", (128, 8 * 2048), BF16)
            w2t = sb(st4, "w2t", (128, 16 * 1024), BF16)
            hesb = sb(st4, "hesb", (128, 16 * 512), BF16)
            hefs = sb(st4, "hefs", (128, 512))
            xfh = sb(st4, "xfh", (128, 8 * 1024), BF16)
            faccs = sb(st4, "faccs", (128, 8 * 1024))
            lgsb = sb(st4, "lgsb", (8, 1024))
            lgtm = sb(st4, "lgtm", (128, 8))
            etm = sb(st4, "etm", (128, 8))
            gatm = sb(st4, "gatm", (128, 8))
            msk = sb(st4, "msk", (128, 8))
            gfm = sb(st4, "gfm", (8, 1024))
            gfmb = sb(st4, "gfmb", (8, 1024), BF16)
            sstat = sb(st4, "sstat", (128, 4))
            gw2b = sb(st4, "gw2b", (128, 16 * 8), BF16)
            be2b = sb(st4, "be2b", (8, DM), BF16)
            gb1sb = sb(st4, "gb1sb", (128, 16))
            bs1sb = sb(st4, "bs1sb", (128, 16))
            bs2sb = sb(st4, "bs2sb", (128, 8))
            be1sb = sb(st4, "be1sb", (128, 8 * 16))
            ftm = sb(st4, "ftm", (128, DM))
            x2tm = sb(st4, "x2tm", (128, DM))
            x1rd = sb(st4, "x1rd", (128, DM))
            sqf = sb(st4, "sqf", (128, DM))
            statf = sb(st4, "statf", (128, 8))
            ind8 = sb(st4, "ind8", (8, 8 * 128))
            p_mm = [psum(st4, f"p5m{i}", (128, 512)) for i in range(4)]
            p_gb = psum(st4, "p5g", (128, 512))
            p_tr = psum(st4, "p5t", (128, 128))

            dma("sync", [(ind8[:, :], ind32_f[0:8, 0:8 * 128])], tag="lind8",
                writes=["ind8b"])

            dma("sync", [(gb1sb[:], gb1_t[:, :]), (bs1sb[:], bs1_t[:, :]),
                         (bs2sb[:], bs2_t[:, :]),
                         (be1sb[:].rearrange("p (a b) -> p a b", a=8),
                          dap(be1_t, 0, [[16, 128], [2048, 8], [1, 16]]))],
                tag="lffb", writes=["ffb"])
            cast_w("gw2", gw2[:, :], gw2b, 16, 8)
            dma("sync", [(wstg[0:8, 0:DM], be2[:, :])], tag="lbe2",
                writes=["wstg"])
            comp("vector", lambda e: e.tensor_copy(out=be2b[:, :],
                                                   in_=wstg[0:8, 0:DM]),
                 tag="cbe2", reads=["wstg"], writes=["w_be2"])

            def stream_w1(name, srch, soff):
                for k in range(8):
                    dma("sync", [(w1t[:, k * 2048:(k + 1) * 2048],
                                  dap(srch, soff + k * 128 * 2048,
                                      [[2048, 128], [1, 2048]]))],
                        tag=f"lw1{name}{k}", writes=["w1t"])

            def stream_w2(name, srch, soff):
                for k in range(16):
                    dma("sync", [(w2t[:, k * 1024:(k + 1) * 1024],
                                  dap(srch, soff + k * 128 * 1024,
                                      [[1024, 128], [1, 1024]]))],
                        tag=f"lw2{name}{k}", writes=["w2t"])

            outsteps = []
            for hf in range(2):
                dma("sync", [(xfh[:].rearrange("p (a t) -> p a t", a=8),
                              dap(xf_spill, hf * 1024,
                                  [[TOWN, 128], [128 * TOWN, 8], [1, 1024]]))],
                    tag=f"lxf{hf}", writes=["xfh"])

                stream_w1(f"g{hf}", gw1_bf, 0)
                for t2 in range(2):
                    for m in range(16):
                        def f_m1(e, t2=t2, m=m):
                            last = None
                            for k in range(8):
                                last = e.matmul(
                                    p_mm[m % 2][:],
                                    w1t[:, k * 2048 + m * 128:
                                        k * 2048 + (m + 1) * 128],
                                    xfh[:, k * 1024 + t2 * 512:
                                        k * 1024 + (t2 + 1) * 512],
                                    start=(k == 0), stop=(k == 7))
                            return last
                        comp("tensor", f_m1, tag=f"gm1_{hf}_{t2}_{m}",
                             reads=["xfh", "w1t"], writes=[f"p5m{m % 2}"])
                        comp("scalar", lambda e, m=m: e.activation(
                            out=hesb[:, m * 512:(m + 1) * 512],
                            in_=p_mm[m % 2][:], func=AF.Gelu,
                            bias=gb1sb[:, m:m + 1]),
                            tag=f"gge_{hf}_{t2}_{m}",
                            reads=[f"p5m{m % 2}", "ffb"], writes=["hesb"])

                    def f_lg(e, t2=t2):
                        last = None
                        for k in range(16):
                            last = e.matmul(
                                p_mm[2][0:8, :],
                                gw2b[0:128, k * 8:(k + 1) * 8],
                                hesb[:, k * 512:(k + 1) * 512],
                                start=(k == 0), stop=(k == 15))
                        return last
                    comp("tensor", f_lg, tag=f"glg{hf}_{t2}",
                         reads=["hesb", "w_gw2"], writes=["p5m2"])
                    comp("vector", lambda e, t2=t2: e.tensor_scalar(
                        out=lgsb[0:8, t2 * 512:(t2 + 1) * 512],
                        in0=p_mm[2][0:8, :], scalar1=gb2s[0:8, 0:1],
                        scalar2=None, op0=ALU.add),
                        tag=f"glge{hf}_{t2}", reads=["p5m2", "consts"],
                        writes=["lgsb"])

                for lb in range(8):
                    def f_tl(e, lb=lb):
                        return e.transpose(
                            p_tr[0:128, 0:8],
                            lgsb[0:8, lb * BLK:(lb + 1) * BLK],
                            identf[0:8, 0:8])
                    comp("tensor", f_tl, tag=f"tl{hf}_{lb}",
                         reads=["lgsb", "consts"], writes=["p5t"])
                    comp("scalar", lambda e: e.copy(out=lgtm[:],
                                                    in_=p_tr[0:128, 0:8]),
                         tag=f"tle{hf}_{lb}", reads=["p5t"], writes=["lgtm"])

                    comp("vector", lambda e: e.reduce_max(
                        out=sstat[:, 0:1], in_=lgtm[:], axis=AX.X,
                        negate=True),
                        tag=f"sm1{hf}_{lb}", reads=["lgtm"],
                        writes=["sstat"])
                    comp("scalar", lambda e: e.activation(
                        out=etm[:], in_=lgtm[:], func=AF.Exp,
                        bias=sstat[:, 0:1]),
                        tag=f"sm2{hf}_{lb}", reads=["lgtm", "sstat"],
                        writes=["etm"])

                    def f_sm3(e):
                        e.reduce_sum(out=sstat[:, 1:2], in_=etm[:], axis=AX.X)
                        e.drain()
                        return e.reciprocal(out=sstat[:, 2:3],
                                            in_=sstat[:, 1:2])
                    comp("vector", f_sm3, tag=f"sm3{hf}_{lb}",
                         reads=["etm"], writes=["sstat"])

                    def f_top2(e):
                        e.tensor_scalar_mul(out=etm[:], in0=etm[:],
                                            scalar1=sstat[:, 2:3])
                        e.drain()
                        e.reduce_max(out=sstat[:, 0:1], in_=etm[:], axis=AX.X)
                        e.drain()
                        e.tensor_scalar(out=msk[:], in0=etm[:],
                                        scalar1=sstat[:, 0:1], scalar2=-1.0,
                                        op0=ALU.is_ge, op1=ALU.mult)
                        e.drain()
                        e.tensor_scalar(out=msk[:], in0=msk[:], scalar1=1.0,
                                        scalar2=None, op0=ALU.add)
                        e.drain()
                        e.tensor_mul(out=gatm[:], in0=etm[:], in1=msk[:])
                        e.drain()
                        e.reduce_max(out=sstat[:, 3:4], in_=gatm[:],
                                     axis=AX.X)
                        e.drain()
                        e.tensor_scalar(out=msk[:], in0=etm[:],
                                        scalar1=sstat[:, 3:4], scalar2=None,
                                        op0=ALU.is_ge)
                        e.drain()
                        return e.tensor_mul(out=gatm[:], in0=etm[:],
                                            in1=msk[:])
                    comp("vector", f_top2, tag=f"tp{hf}_{lb}",
                         reads=["etm", "sstat"], writes=["gatm", "sstat"])

                    def f_tg(e):
                        return e.transpose(p_tr[0:8, 0:128], gatm[:],
                                           identf[:])
                    comp("tensor", f_tg, tag=f"tg{hf}_{lb}",
                         reads=["gatm", "consts"], writes=["p5t"])
                    comp("scalar", lambda e, lb=lb: e.copy(
                        out=gfm[0:8, lb * BLK:(lb + 1) * BLK],
                        in_=p_tr[0:8, 0:128]),
                        tag=f"tge{hf}_{lb}", reads=["p5t"], writes=["gfm"])
                comp("vector", lambda e: e.tensor_copy(out=gfmb[:, :],
                                                       in_=gfm[0:8, :]),
                     tag=f"cgf{hf}", reads=["gfm"], writes=["gfmb"])


                def mlp(name, w1h, w1off, w2h, w2off, b1col, gated_e,
                        is_shared):
                    stream_w1(name, w1h, w1off)
                    stream_w2(name, w2h, w2off)
                    for t2 in range(2):
                        if gated_e is not None:
                            def f_gb(e, t2=t2, ge=gated_e):
                                return e.matmul(
                                    p_gb[:],
                                    ind8[0:8, ge * 128:(ge + 1) * 128],
                                    gfm[0:8, t2 * 512:(t2 + 1) * 512],
                                    start=True, stop=True)
                            comp("tensor", f_gb, tag=f"gb{name}{t2}",
                                 reads=["gfm", "ind8b"], writes=["p5g"])
                        for m in range(16):
                            def f_m1(e, t2=t2, m=m):
                                last = None
                                for k in range(8):
                                    last = e.matmul(
                                        p_mm[m % 2][:],
                                        w1t[:, k * 2048 + m * 128:
                                            k * 2048 + (m + 1) * 128],
                                        xfh[:, k * 1024 + t2 * 512:
                                            k * 1024 + (t2 + 1) * 512],
                                        start=(k == 0), stop=(k == 7))
                                return last
                            comp("tensor", f_m1, tag=f"m1{name}{t2}_{m}",
                                 reads=["xfh", "w1t"], writes=[f"p5m{m % 2}"])
                            if gated_e is None:
                                def f_ge(e, m=m, b1col=b1col):
                                    return e.activation(
                                        out=hesb[:, m * 512:(m + 1) * 512],
                                        in_=p_mm[m % 2][:],
                                        func=AF.Gelu_apprx_tanh,
                                        bias=b1col(m))
                                comp("scalar", f_ge, tag=f"ge{name}{t2}_{m}",
                                     reads=[f"p5m{m % 2}", "ffb"],
                                     writes=["hesb"])
                            else:
                                def f_ge(e, m=m, b1col=b1col):
                                    return e.activation(
                                        out=hefs[:], in_=p_mm[m % 2][:],
                                        func=AF.Gelu_apprx_tanh,
                                        bias=b1col(m))
                                comp("scalar", f_ge, tag=f"ge{name}{t2}_{m}",
                                     reads=[f"p5m{m % 2}", "ffb"],
                                     writes=["hefs"])
                                comp("vector", lambda e, m=m: e.tensor_mul(
                                    out=hesb[:, m * 512:(m + 1) * 512],
                                    in0=hefs[:], in1=p_gb[:]),
                                    tag=f"gm{name}{t2}_{m}",
                                    reads=["hefs", "p5g"], writes=["hesb"])
                        for fo in range(8):
                            def f_m2(e, fo=fo):
                                last = None
                                for k in range(16):
                                    last = e.matmul(
                                        p_mm[2 + fo % 2][:],
                                        w2t[:, k * 1024 + fo * 128:
                                            k * 1024 + (fo + 1) * 128],
                                        hesb[:, k * 512:(k + 1) * 512],
                                        start=(k == 0), stop=(k == 15))
                                return last
                            comp("tensor", f_m2, tag=f"m2{name}{t2}_{fo}",
                                 reads=["hesb", "w2t"],
                                 writes=[f"p5m{2 + fo % 2}"])
                            fsl = faccs[:, fo * 1024 + t2 * 512:
                                        fo * 1024 + (t2 + 1) * 512]
                            if is_shared:
                                comp("vector", lambda e, fo=fo, fsl=fsl:
                                     e.tensor_scalar(
                                         out=fsl, in0=p_mm[2 + fo % 2][:],
                                         scalar1=bs2sb[:, fo:fo + 1],
                                         scalar2=None, op0=ALU.add),
                                     tag=f"fa{name}{t2}_{fo}",
                                     reads=[f"p5m{2 + fo % 2}", "ffb"],
                                     writes=["faccs"])
                            else:
                                comp("vector", lambda e, fo=fo, fsl=fsl:
                                     e.tensor_add(out=fsl, in0=fsl,
                                                  in1=p_mm[2 + fo % 2][:]),
                                     tag=f"fa{name}{t2}_{fo}",
                                     reads=[f"p5m{2 + fo % 2}", "faccs"],
                                     writes=["faccs"])

                mlp(f"s{hf}", ws1_bf, 0, ws2_bf, 0,
                    lambda m: bs1sb[:, m:m + 1], None, True)
                for ee in range(8):
                    mlp(f"h{hf}e{ee}", we1_bf, ee * DM * 2048,
                        we2_bf, ee * 2048 * DM,
                        (lambda m, ee=ee:
                         be1sb[:, ee * 16 + m:ee * 16 + m + 1]),
                        ee, False)

                for fo in range(8):
                    for t2 in range(2):
                        def f_b2(e, fo=fo, t2=t2):
                            return e.matmul(
                                p_mm[fo % 2][:],
                                be2b[0:8, fo * 128:(fo + 1) * 128],
                                gfmb[0:8, t2 * 512:(t2 + 1) * 512],
                                start=True, stop=True)
                        comp("tensor", f_b2, tag=f"b2{hf}_{fo}_{t2}",
                             reads=["w_be2", "gfmb"], writes=[f"p5m{fo % 2}"])
                        fsl = faccs[:, fo * 1024 + t2 * 512:
                                    fo * 1024 + (t2 + 1) * 512]
                        comp("vector", lambda e, fsl=fsl, fo=fo:
                             e.tensor_add(out=fsl, in0=fsl,
                                          in1=p_mm[fo % 2][:]),
                             tag=f"b2a{hf}_{fo}_{t2}",
                             reads=[f"p5m{fo % 2}", "faccs"],
                             writes=["faccs"])

                for lb in range(8):
                    b = hf * 8 + lb
                    for ft in range(8):
                        def f_tf(e, lb=lb, ft=ft):
                            return e.transpose(
                                p_tr[:],
                                faccs[:, ft * 1024 + lb * BLK:
                                      ft * 1024 + (lb + 1) * BLK],
                                identf[:])
                        comp("tensor", f_tf, tag=f"tf{b}_{ft}",
                             reads=["faccs", "consts"], writes=["p5t"])
                        comp("scalar", lambda e, ft=ft: e.copy(
                            out=ftm[:, ft * 128:(ft + 1) * 128],
                            in_=p_tr[:]),
                            tag=f"ef{b}_{ft}", reads=["p5t"], writes=["ftm"])
                    dma("sync", [(x1rd[:],
                                  x1_spill[b * BLK:(b + 1) * BLK, :])],
                        tag=f"lx1{b}", writes=["x1rd"])

                    def f_a5(e):
                        e.reduce_sum(out=statf[:, 0:1], in_=ftm[:], axis=AX.X)
                        e.drain()
                        e.tensor_scalar_mul(out=statf[:, 0:1],
                                            in0=statf[:, 0:1], scalar1=RD)
                        e.drain()
                        return e.tensor_scalar(out=x2tm[:], in0=ftm[:],
                                               scalar1=statf[:, 0:1],
                                               scalar2=None,
                                               op0=ALU.subtract)
                    comp("vector", f_a5, tag=f"l5a{b}", reads=["ftm"],
                         writes=["x2tm", "statf"])
                    comp("scalar", lambda e: e.activation(
                        out=sqf[:], in_=x2tm[:], func=AF.Square),
                        tag=f"l5s{b}", reads=["x2tm"], writes=["sqf"])

                    def f_v5(e):
                        e.reduce_sum(out=statf[:, 1:2], in_=sqf[:], axis=AX.X)
                        e.drain()
                        return e.tensor_scalar_mul(out=statf[:, 1:2],
                                                   in0=statf[:, 1:2],
                                                   scalar1=RD)
                    comp("vector", f_v5, tag=f"l5v{b}", reads=["sqf"],
                         writes=["statf"])

                    def f_r5(e):
                        e.activation(out=statf[:, 2:3], in_=statf[:, 1:2],
                                     func=AF.Ln, bias=epsc[:, 0:1])
                        return e.activation(out=statf[:, 3:4],
                                            in_=statf[:, 2:3], func=AF.Exp,
                                            scale=-0.5)
                    comp("scalar", f_r5, tag=f"l5r{b}", reads=["statf"],
                         writes=["statf"])

                    def f_x2(e):
                        e.tensor_scalar_mul(out=x2tm[:], in0=x2tm[:],
                                            scalar1=statf[:, 3:4])
                        e.drain()
                        return e.tensor_add(out=x2tm[:], in0=x2tm[:],
                                            in1=x1rd[:])
                    comp("vector", f_x2, tag=f"x2{b}",
                         reads=["x2tm", "x1rd", "statf"], writes=["x2tm"])
                    s = dma("sync", [(out_own.ap()[b * BLK:(b + 1) * BLK, :],
                                      x2tm[:])], tag=f"so{b}",
                            reads=["x2tm"])
                    outsteps.append(s)

            for nm in debug_names:
                h, shape, dt = internals[nm]
                s = dma("gpsimd", [(dbg_outs[nm].ap()[:, :], h.ap()[:, :])],
                        deps=[sc.steps[-1]], tag=f"dbg_{nm}")
                outsteps.append(s)

            comp("sync", lambda e: e.nop(), deps=outsteps, tag="fin")

        coll_sems = {n: es.enter_context(nc.semaphore(n))
                     for n in sc.coll_names}
        with nc.Block() as block:
            block.sync(lambda eng: sc.emit_for(sems, coll_sems, "sync", eng))
            block.vector(lambda eng: sc.emit_for(sems, coll_sems, "vector",
                                                 eng))
            block.scalar(lambda eng: sc.emit_for(sems, coll_sems, "scalar",
                                                 eng))
            block.tensor(lambda eng: sc.emit_for(sems, coll_sems, "tensor",
                                                 eng))
            block.gpsimd(lambda eng: sc.emit_for(sems, coll_sems, "gpsimd",
                                                 eng))
    return nc


def _ind32():
    ind = np.zeros((32, 32 * 128), dtype=np.float32)
    for j in range(32):
        ind[j, j * 128:(j + 1) * 128] = 1.0
    return ind


def host_inputs(inputs):
    g = lambda k: np.asarray(inputs[k])
    x = np.ascontiguousarray(g("L0_attn_x")[0])
    xb = x.reshape(T // BLK, BLK, DM)
    import ml_dtypes
    shared = {
        "gamma_t": None, "beta_t": None,
    }

    def tcol(v):
        k = v.shape[0] // 128
        return np.ascontiguousarray(v.reshape(k, 128).T)

    in_w = g("L0_attn_in_w")
    merge_w = g("L0_attn_merge_w")
    perm = np.concatenate([
        np.concatenate([np.arange(r * DSL, (r + 1) * DSL),
                        np.arange(DIN + r * DSL, DIN + (r + 1) * DSL)])
        for r in range(NCORES)])
    merge_w_p = np.ascontiguousarray(
        merge_w[perm].reshape(32, 128, 16, 128).transpose(2, 1, 0, 3)
        .reshape(16 * 128, 32 * 128))

    shared = {
        "gamma_t": tcol(g("L0_attn_gamma")),
        "beta_t": tcol(g("L0_attn_beta")),
        "in_w_z": np.ascontiguousarray(in_w[:, DIN:]),
        "ident_f": np.eye(128, dtype=np.float32),
        "ident_b": np.eye(128, dtype=ml_dtypes.bfloat16),
        "ones_f": np.ones((1, 128), dtype=np.float32),
        "ind32_f": _ind32(),
        "merge_w_p": merge_w_p,
        "out_w": g("L0_attn_out_w"),
        "gw1": g("L0_ffn_gw1"), "gb1_t": tcol(g("L0_ffn_gb1")),
        "gw2": g("L0_ffn_gw2"),
        "gb2c": np.ascontiguousarray(g("L0_ffn_gb2").reshape(8, 1)),
        "ws1": g("L0_ffn_Ws1"), "bs1_t": tcol(g("L0_ffn_bs1")),
        "ws2": g("L0_ffn_Ws2"), "bs2_t": tcol(g("L0_ffn_bs2")),
        "we1": g("L0_ffn_We1"),
        "be1_t": np.ascontiguousarray(
            g("L0_ffn_be1").reshape(8, 16, 128).transpose(0, 2, 1)),
        "we2": g("L0_ffn_We2"), "be2": g("L0_ffn_be2"),
    }

    def dsl_(v, j):
        return np.ascontiguousarray(v[j * DSL:(j + 1) * DSL])

    def dtile(v, j):
        return np.ascontiguousarray(dsl_(v, j).reshape(2, 128).T)

    in_maps = []
    for j in range(NCORES):
        m = dict(shared)
        m["x_own"] = np.ascontiguousarray(x[j * TOWN:(j + 1) * TOWN])
        m["in_w_xs"] = np.ascontiguousarray(in_w[:, j * DSL:(j + 1) * DSL])
        for d, sfx in ((0, ""), (1, "_b")):
            m[f"conv_w_t{d}"] = np.ascontiguousarray(
                dsl_(g(f"L0_attn_conv_w{sfx}"), j).reshape(
                    2, 128, 4).transpose(1, 0, 2))
            m[f"conv_b_t{d}"] = dtile(g(f"L0_attn_conv_b{sfx}"), j)
            m[f"dskip_t{d}"] = dtile(g(f"L0_attn_D_skip{sfx}"), j)
            m[f"dtb_t{d}"] = dtile(g(f"L0_attn_dt_b{sfx}"), j)
            m[f"alog_t{d}"] = np.ascontiguousarray(
                dsl_(g(f"L0_attn_A_log{sfx}"), j).reshape(
                    2, 128, NST).transpose(1, 0, 2))
            m[f"xproj_w{d}"] = dsl_(g(f"L0_attn_xproj_w{sfx}"), j)
            m[f"dt_w{d}"] = np.ascontiguousarray(
                g(f"L0_attn_dt_w{sfx}")[:, j * DSL:(j + 1) * DSL])
        in_maps.append(m)
    return in_maps


def run(inputs, debug_names=(), **kw):
    in_maps = host_inputs(inputs)
    nc = build_program(debug_names=tuple(debug_names))
    res = run_bass_kernel_spmd(nc, in_maps, core_ids=list(range(NCORES)), **kw)
    out = np.concatenate([res.results[j]["out_own"] for j in range(NCORES)],
                         axis=0)
    return np.ascontiguousarray(out).reshape(1, T, DM), res


def kernel(**inputs):
    out, _ = run(inputs)
    return out


# version 09 (final)
# speedup vs baseline: 1.1872x; 1.1872x over previous
import numpy as np
from contextlib import ExitStack

import concourse.bass as bass
import concourse.mybir as mybir
from concourse.bass_utils import run_bass_kernel_spmd

F32 = mybir.dt.float32
BF16 = mybir.dt.bfloat16
AF = mybir.ActivationFunctionType
ALU = mybir.AluOpType
AX = mybir.AxisListType

NCORES = 8
T = 16384
DM = 1024
DIN = 2048
DSL = DIN // NCORES
NST = 16
DTR = 64
TOWN = T // NCORES
BLK = 128
NBLK = TOWN // BLK
CH = 1024
NCH = T // CH
CCH = 2048
EPS = 1e-5
RD = 1.0 / DM

ENGINES = ("sync", "vector", "scalar", "tensor", "gpsimd")


class Step:
    __slots__ = ("eng", "fn", "deps", "idx", "seq", "n_dma", "coll", "tag",
                 "semref", "extra_waits")

    def __init__(self, eng, fn, deps, n_dma=0, coll=None, tag=""):
        self.eng, self.fn, self.deps = eng, fn, deps
        self.n_dma, self.coll, self.tag = n_dma, coll, tag
        self.idx = self.seq = None
        self.semref = None
        self.extra_waits = None


NDMASEM = 32


class Builder:
    def __init__(self):
        self.steps = []
        self.eng_count = {e: 0 for e in ENGINES}
        self.dma_step_count = {e: 0 for e in ENGINES}
        self.dma_sem_total = {(e, i): 0 for e in ENGINES
                              for i in range(NDMASEM)}
        self.coll_names = []
        self.bufs = {}

    def buf(self, name):
        if name not in self.bufs:
            self.bufs[name] = [None, []]
        return self.bufs[name]

    def _mk(self, eng, fn, deps, n_dma, coll, tag, reads, writes):
        deps = [d for d in deps if d is not None]
        cname = None
        if coll:
            cname = f"cc{len(self.coll_names)}"
            self.coll_names.append(cname)
        s = Step(eng, fn, deps, n_dma=n_dma, coll=cname, tag=tag)
        s.idx = len(self.steps)
        if n_dma:
            i = self.dma_step_count[eng] % NDMASEM
            self.dma_step_count[eng] += 1
            prev = self.dma_sem_total[(eng, i)]
            self.dma_sem_total[(eng, i)] = prev + n_dma * 16
            s.seq = (i, prev, prev + n_dma * 16)
        elif cname is None:
            self.eng_count[eng] += 1
            s.seq = self.eng_count[eng]
        else:
            s.seq = 1
        self.steps.append(s)
        for b in reads:
            bb = self.buf(b)
            if bb[0] is not None:
                s.deps.append(bb[0])
            bb[1].append(s)
        for b in writes:
            bb = self.buf(b)
            if bb[0] is not None and bb[0] is not s:
                s.deps.append(bb[0])
            s.deps.extend(x for x in bb[1] if x is not s)
            bb[0], bb[1] = s, []
        return s

    def dma(self, eng, pairs, deps=(), tag="", reads=(), writes=()):
        semref = {}

        def fn(e, pairs=pairs, semref=semref):
            last = None
            for dst, src in pairs:
                last = e.dma_start(out=dst, in_=src).then_inc(semref["s"], 16)
            return last
        s = self._mk(eng, fn, list(deps), len(pairs), False, tag, reads,
                     writes)
        s.semref = semref
        return s

    def comp(self, eng, fn, deps=(), tag="", reads=(), writes=(), coll=False):
        return self._mk(eng, fn, list(deps), 0, coll, tag, reads, writes)

    def barrier(self):
        lasts = []
        for e in ENGINES:
            es_ = [s for s in self.steps if s.eng == e]
            if es_:
                lasts.append(es_[-1])
        extra = {(e, "dma", i): tot
                 for (e, i), tot in self.dma_sem_total.items() if tot > 0}
        bsteps = []
        for e in ENGINES:
            s = self._mk(e, lambda eng: eng.nop(), list(lasts), 0, False,
                         f"bar_{e}", (), ())
            s.extra_waits = dict(extra)
            bsteps.append(s)
        for name, bb in self.bufs.items():
            bb[0], bb[1] = bsteps[0], []
        return bsteps

    def emit_for(self, sems, coll_sems, eng_name, eng):
        hw = {}
        for s in self.steps:
            if s.eng != eng_name:
                continue
            waits = {}
            for d in s.deps:
                if d.coll is not None:
                    key = ("coll", d.coll)
                    val = d.seq
                elif d.n_dma:
                    key = (d.eng, "dma", d.seq[0])
                    val = d.seq[2]
                else:
                    key = (d.eng, "seq")
                    val = d.seq
                waits[key] = max(waits.get(key, 0), val)
            if s.n_dma and s.seq[1] > 0:
                k = (s.eng, "dma", s.seq[0])
                waits[k] = max(waits.get(k, 0), s.seq[1])
            if s.extra_waits:
                for k, v in s.extra_waits.items():
                    waits[k] = max(waits.get(k, 0), v)
            for key in sorted(waits, key=repr):
                if waits[key] <= hw.get(key, 0):
                    continue
                hw[key] = waits[key]
                if key[0] == "coll":
                    eng.wait_ge(coll_sems[key[1]], waits[key])
                else:
                    eng.wait_ge(sems[key], waits[key])
            if s.semref is not None:
                s.semref["s"] = sems[(s.eng, "dma", s.seq[0])]
            last = s.fn(eng)
            if s.coll is not None:
                last.then_inc(coll_sems[s.coll])
            elif s.n_dma:
                pass
            else:
                if last is None:
                    last = eng.nop()
                last.then_inc(sems[(s.eng, "seq")], 1)


def dap(handle, offset, dims):
    return bass.AP(handle, offset, dims)


def build_program(debug_names=()):
    nc = bass.Bass("TRN2", target_bir_lowering=False)
    sc = Builder()
    dma, comp = sc.dma, sc.comp

    def din(name, shape, dt=F32):
        return nc.dram_tensor(name, list(shape), dt, kind="ExternalInput")

    def dint(name, shape, dt=F32):
        return nc.dram_tensor(name, list(shape), dt, kind="Internal")

    x_own = din("x_own", (TOWN, DM))
    gamma_t = din("gamma_t", (128, 8))
    beta_t = din("beta_t", (128, 8))
    in_w_xs = din("in_w_xs", (DM, DSL))
    in_w_z = din("in_w_z", (DM, DIN))
    ident_f = din("ident_f", (128, 128))
    ident_b = din("ident_b", (128, 128), BF16)
    ones_f = din("ones_f", (1, 128))
    ind32_f = din("ind32_f", (32, 32 * 128))
    conv_w_t = [din(f"conv_w_t{d}", (128, 2, 4)) for d in range(2)]
    conv_b_t = [din(f"conv_b_t{d}", (128, 2)) for d in range(2)]
    dskip_t = [din(f"dskip_t{d}", (128, 2)) for d in range(2)]
    dtb_t = [din(f"dtb_t{d}", (128, 2)) for d in range(2)]
    alog_t = [din(f"alog_t{d}", (128, 2, NST)) for d in range(2)]
    xproj_w = [din(f"xproj_w{d}", (DSL, 96)) for d in range(2)]
    dt_w = [din(f"dt_w{d}", (DTR, DSL)) for d in range(2)]
    merge_w_p = din("merge_w_p", (16 * 128, 32 * 128))
    out_w = din("out_w", (DIN, DM))
    gw1 = din("gw1", (DM, 2048))
    gb1_t = din("gb1_t", (128, 16))
    gw2 = din("gw2", (2048, 8))
    gb2c = din("gb2c", (8, 1))
    ws1 = din("ws1", (DM, 2048))
    bs1_t = din("bs1_t", (128, 16))
    ws2 = din("ws2", (2048, DM))
    bs2_t = din("bs2_t", (128, 8))
    we1 = din("we1", (8, DM, 2048))
    be1_t = din("be1_t", (8, 128, 16))
    we2 = din("we2", (8, 2048, DM))
    be2 = din("be2", (8, DM))

    out_own = nc.dram_tensor("out_own", [TOWN, DM], F32, kind="ExternalOutput")

    XSW = T + 8
    internals = {}

    def idram(name, shape, dt=F32):
        h = dint(name, shape, dt)
        internals[name] = (h, shape, dt)
        return h

    ag_h2_in = idram("ag_h2_in", (DM, TOWN), BF16)
    ag_h2_out = idram("ag_h2_out", (NCORES * DM, TOWN), BF16)
    ar_in = [idram(f"ar_in{d}", (96, T)) for d in range(2)]
    ar_out = [idram(f"ar_out{d}", (96, T)) for d in range(2)]
    ag_y_in = [idram(f"ag_y_in{c}", (2 * DSL, CH), BF16) for c in range(NCH)]
    ag_y_out = [idram(f"ag_y_out{c}", (NCORES * 2 * DSL, CH), BF16)
                for c in range(NCH)]
    xs_spill = idram("xs_spill", (DSL, XSW))
    xc_spill = [idram(f"xc_spill{d}", (DSL, T)) for d in range(2)]
    z_spill = idram("z_spill", (DIN, TOWN))
    u1_spill = idram("u1_spill", (DM, TOWN))
    x1_spill = idram("x1_spill", (TOWN, DM))
    yf_spill = idram("yf_spill", (DSL, T), BF16)
    mw_bf = idram("mw_bf", (16 * 128, 32 * 128), BF16)
    xf_spill = idram("xf_spill", (DM, TOWN), BF16)
    gw1_bf = idram("gw1_bf", (DM, 2048), BF16)
    ws1_bf = idram("ws1_bf", (DM, 2048), BF16)
    ws2_bf = idram("ws2_bf", (2048, DM), BF16)
    we1_bf = idram("we1_bf", (8 * DM, 2048), BF16)
    we2_bf = idram("we2_bf", (8 * 2048, DM), BF16)


    dbg_outs = {}
    for nm in debug_names:
        h, shape, dt = internals[nm]
        dbg_outs[nm] = nc.dram_tensor(f"dbg_{nm}", list(shape), dt,
                                      kind="ExternalOutput")

    es = ExitStack()

    def sb(st, name, shape, dt=F32):
        return st.enter_context(nc.sbuf_tensor(name, list(shape), dt))

    def psum(st, name, shape, dt=F32):
        return st.enter_context(nc.psum_tensor(name, list(shape), dt))

    with es:
        sems = {}
        for e in ENGINES:
            sems[(e, "seq")] = es.enter_context(nc.semaphore(f"sq_{e}"))
        for e in ("sync", "gpsimd"):
            for i in range(NDMASEM):
                sems[(e, "dma", i)] = es.enter_context(
                    nc.semaphore(f"dm_{e}{i}"))

        identf = sb(es, "identf", (128, 128))
        identb = sb(es, "identb", (128, 128), BF16)
        onesf = sb(es, "onesf", (1, 128))
        gam = sb(es, "gam", (128, 8))
        bet = sb(es, "bet", (128, 8))
        convw = sb(es, "convw", (128, 16))
        convb = sb(es, "convb", (128, 4))
        dskip = sb(es, "dskip", (128, 4))
        dtbias = sb(es, "dtbias", (128, 4))
        negA = sb(es, "negA", (128, 64))
        gb2s = sb(es, "gb2s", (8, 1))
        xprojw_b = sb(es, "xprojw_b", (128, 2 * 2 * 96), BF16)
        dtw_b = sb(es, "dtw_b", (64, 2 * 2 * 128), BF16)
        wstg = sb(es, "wstg", (128, 2048))
        pwst = [sb(es, f"pwst{i}", (128, 1024)) for i in range(2)]
        pbst = [sb(es, f"pbst{i}", (128, 1024), BF16) for i in range(2)]
        evac = sb(es, "evac", (128, 512))
        evac2 = sb(es, "evac2", (128, 512))
        evacbf = sb(es, "evacbf", (128, 2048), BF16)
        epsc = sb(es, "epsc", (128, 1))
        pidsb = sb(es, "pidsb", (1, 1), mybir.dt.uint32)

        def cast_w(name, src, dst_sb, kt, mcols, kdim=128, dst_off=0,
                   deps=()):
            for k in range(kt):
                dma("sync", [(wstg[0:kdim, 0:mcols],
                              src[k * kdim:(k + 1) * kdim, :])],
                    deps=deps, tag=f"lw_{name}{k}", writes=["wstg"])

                def fc(e, k=k, mcols=mcols, kdim=kdim, dst_sb=dst_sb,
                       dst_off=dst_off):
                    return e.tensor_copy(
                        out=dst_sb[0:kdim, dst_off + k * mcols:
                                   dst_off + (k + 1) * mcols],
                        in_=wstg[0:kdim, 0:mcols])
                comp("vector", fc, tag=f"cw_{name}{k}", reads=["wstg"],
                     writes=[f"w_{name}"])

        dma("sync", [
            (identf[:], ident_f[:, :]), (identb[:], ident_b[:, :]),
            (onesf[:], ones_f[:, :]), (gam[:], gamma_t[:, :]),
            (bet[:], beta_t[:, :]), (gb2s[:], gb2c[:, :]),
        ], tag="ldc", writes=["consts"])
        dma("sync", [
            (convw[:, 0:8], conv_w_t[0][:, :, :].rearrange("p a b -> p (a b)")),
            (convw[:, 8:16], conv_w_t[1][:, :, :].rearrange("p a b -> p (a b)")),
            (convb[:, 0:2], conv_b_t[0][:, :]),
            (convb[:, 2:4], conv_b_t[1][:, :]),
            (dskip[:, 0:2], dskip_t[0][:, :]),
            (dskip[:, 2:4], dskip_t[1][:, :]),
            (dtbias[:, 0:2], dtb_t[0][:, :]),
            (dtbias[:, 2:4], dtb_t[1][:, :]),
            (negA[:, 0:32], alog_t[0][:, :, :].rearrange("p a b -> p (a b)")),
            (negA[:, 32:64], alog_t[1][:, :, :].rearrange("p a b -> p (a b)")),
        ], tag="lds", writes=["smallw"])
        comp("vector", lambda e: e.memset(epsc[:], EPS), tag="mseps",
             writes=["consts"])
        dma("sync", [(pidsb[:], nc.partition_id_tensor.ap()[:, :])],
            tag="lpid", writes=["pidb"])

        comp("scalar", lambda e: e.activation(out=negA[:], in_=negA[:],
                                              func=AF.Exp),
             tag="negA_e", reads=["smallw"], writes=["smallw"])
        comp("vector", lambda e: e.tensor_scalar_mul(out=negA[:], in0=negA[:],
                                                     scalar1=-1.0),
             tag="negA_n", reads=["smallw"], writes=["smallw"])
        for d in range(2):
            cast_w(f"xp{d}", xproj_w[d][:, :], xprojw_b, 2, 96,
                   dst_off=d * 192)
            cast_w(f"dw{d}", dt_w[d][:, :], dtw_b, 1, 256, kdim=64,
                   dst_off=d * 256)
        for mt in range(16):
            for h in range(2):
                dma("sync", [(wstg[:, 0:2048],
                              dap(merge_w_p, mt * 128 * 4096 + h * 2048,
                                  [[4096, 128], [1, 2048]]))],
                    tag=f"lmw{mt}_{h}", writes=["wstg"])
                comp("vector", lambda e: e.tensor_copy(
                    out=evacbf[:, 0:2048], in_=wstg[:, 0:2048]),
                    tag=f"cmw{mt}_{h}", reads=["wstg"], writes=["evbf"])
                dma("sync", [(dap(mw_bf, mt * 128 * 4096 + h * 2048,
                                  [[4096, 128], [1, 2048]]),
                              evacbf[:, 0:2048])],
                    tag=f"smw{mt}_{h}", reads=["evbf"])

        pcjobs = []
        pcjobs.append((gw1, 0, gw1_bf, 0, 8, 2048))
        pcjobs.append((ws1, 0, ws1_bf, 0, 8, 2048))
        pcjobs.append((ws2, 0, ws2_bf, 0, 16, 1024))
        for ee in range(8):
            pcjobs.append((we1, ee * DM * 2048, we1_bf, ee * DM * 2048,
                           8, 2048))
        for ee in range(8):
            pcjobs.append((we2, ee * 2048 * DM, we2_bf, ee * 2048 * DM,
                           16, 1024))
        pcitems = []
        for srch, soff, dsth, doff, kt, mcols in pcjobs:
            for k in range(kt):
                for h in range(mcols // 1024):
                    off = k * 128 * mcols + h * 1024
                    pcitems.append((srch, soff + off, dsth, doff + off,
                                    mcols))
        pcstate = {"i": 0}

        def pc_drip(nn):
            i0 = pcstate["i"]
            for t in range(i0, min(i0 + nn, len(pcitems))):
                srch, soff, dsth, doff, mcols = pcitems[t]
                par = t % 2
                dma("gpsimd", [(pwst[par][:, :],
                                dap(srch, soff,
                                    [[mcols, 128], [1, 1024]]))],
                    tag=f"pcl{t}", writes=[f"pcw{par}"])
                comp("scalar", lambda e, par=par: e.copy(
                    out=pbst[par][:, :], in_=pwst[par][:, :]),
                    tag=f"pcc{t}", reads=[f"pcw{par}"],
                    writes=[f"pcb{par}"])
                dma("gpsimd", [(dap(dsth, doff,
                                    [[mcols, 128], [1, 1024]]),
                                pbst[par][:, :])],
                    tag=f"pcs{t}", reads=[f"pcb{par}"])
            pcstate["i"] = min(i0 + nn, len(pcitems))

        st1 = ExitStack()
        with st1:
            inw_xs_b = sb(st1, "inw_xs_b", (128, 8 * 256), BF16)
            inw_z_b = sb(st1, "inw_z_b", (128, 8 * 2048), BF16)
            xtm = [sb(st1, f"xtm{i}", (128, DM)) for i in range(2)]
            u1tm = [sb(st1, f"u1tm{i}", (128, DM)) for i in range(2)]
            u2tm = [sb(st1, f"u2tm{i}", (128, DM)) for i in range(2)]
            sq = [sb(st1, f"sq{i}", (128, DM)) for i in range(2)]
            stat = [sb(st1, f"stat{i}", (128, 8)) for i in range(2)]
            h2sb = sb(st1, "h2sb", (128, 8 * TOWN), BF16)
            u1fm = [sb(st1, f"u1fm{i}", (128, 8 * BLK)) for i in range(2)]
            h2rd = sb(st1, "h2rd", (128, 2 * 8 * 512), BF16)
            p1_mm = [psum(st1, f"p1m{i}", (128, 512)) for i in range(4)]
            p1_tr = [psum(st1, f"p1t{i}", (128, 128)) for i in range(2)]

            cast_w("inwxs", in_w_xs[:, :], inw_xs_b, 8, 256)
            cast_w("inwz", in_w_z[:, :], inw_z_b, 8, 2048)

            def ln_stats(src, dst, rbuf, wbuf, tag, pp):
                stp, sqp_ = stat[pp], sq[pp]

                def f_a(e, src=src, dst=dst):
                    e.reduce_sum(out=stp[:, 0:1], in_=src, axis=AX.X)
                    e.drain()
                    e.tensor_scalar_mul(out=stp[:, 0:1],
                                        in0=stp[:, 0:1], scalar1=RD)
                    e.drain()
                    return e.tensor_scalar(out=dst, in0=src,
                                           scalar1=stp[:, 0:1],
                                           scalar2=None, op0=ALU.subtract)
                comp("vector", f_a, tag=f"{tag}a", reads=[rbuf],
                     writes=[wbuf, f"stat{pp}"])
                comp("scalar", lambda e, dst=dst: e.activation(
                    out=sqp_[:], in_=dst, func=AF.Square),
                    tag=f"{tag}s", reads=[wbuf], writes=[f"sq{pp}"])

                def f_v(e):
                    e.reduce_sum(out=stp[:, 1:2], in_=sqp_[:], axis=AX.X)
                    e.drain()
                    return e.tensor_scalar_mul(out=stp[:, 1:2],
                                               in0=stp[:, 1:2], scalar1=RD)
                comp("vector", f_v, tag=f"{tag}v", reads=[f"sq{pp}"],
                     writes=[f"stat{pp}"])

                def f_r(e):
                    e.activation(out=stp[:, 2:3], in_=stp[:, 1:2],
                                 func=AF.Ln, bias=epsc[:, 0:1])
                    return e.activation(out=stp[:, 3:4], in_=stp[:, 2:3],
                                        func=AF.Exp, scale=-0.5)
                comp("scalar", f_r, tag=f"{tag}r", reads=[f"stat{pp}"],
                     writes=[f"stat{pp}"])
                comp("vector", lambda e, dst=dst: e.tensor_scalar_mul(
                    out=dst, in0=dst, scalar1=stp[:, 3:4]),
                    tag=f"{tag}u", reads=[wbuf, f"stat{pp}"], writes=[wbuf])

            for b in range(NBLK):
                bp = b % 2
                dma("sync", [(xtm[bp][:], x_own[b * BLK:(b + 1) * BLK, :])],
                    tag=f"ldx{b}", writes=[f"B_x{bp}"])
                ln_stats(xtm[bp][:], u1tm[bp][:], f"B_x{bp}", f"B_u1{bp}",
                         f"l1_{b}", bp)
                ln_stats(u1tm[bp][:], u2tm[bp][:], f"B_u1{bp}", f"B_u2{bp}",
                         f"l2_{b}", bp)
                for ft in range(8):
                    def f_t1(e, ft=ft, bp=bp):
                        return e.transpose(
                            p1_tr[0][:],
                            u1tm[bp][:, ft * 128:(ft + 1) * 128],
                            identf[:])
                    comp("tensor", f_t1, tag=f"t1_{b}_{ft}",
                         reads=[f"B_u1{bp}", "consts"], writes=["p_tr0"])

                    def f_e1(e, ft=ft, bp=bp):
                        return e.copy(
                            out=u1fm[bp][:, ft * BLK:(ft + 1) * BLK],
                            in_=p1_tr[0][:])
                    comp("scalar", f_e1, tag=f"e1_{b}_{ft}",
                         reads=["p_tr0"], writes=[f"u1fm{bp}"])

                    def f_t2(e, ft=ft, bp=bp):
                        return e.transpose(
                            p1_tr[1][:],
                            u2tm[bp][:, ft * 128:(ft + 1) * 128],
                            identf[:])
                    comp("tensor", f_t2, tag=f"t2_{b}_{ft}",
                         reads=[f"B_u2{bp}", "consts"], writes=["p_tr1"])

                    def f_e2(e, ft=ft, b=b):
                        return e.tensor_scalar(
                            out=h2sb[:, ft * TOWN + b * BLK:
                                     ft * TOWN + (b + 1) * BLK],
                            in0=p1_tr[1][:], scalar1=gam[:, ft:ft + 1],
                            scalar2=bet[:, ft:ft + 1], op0=ALU.mult,
                            op1=ALU.add)
                    comp("vector", f_e2, tag=f"e2_{b}_{ft}",
                         reads=["p_tr1", "consts"], writes=["h2sb"])

                dma("sync", [(dap(u1_spill, b * BLK,
                                  [[TOWN, 128], [128 * TOWN, 8], [1, BLK]]),
                    u1fm[bp][:].rearrange("p (a t) -> p a t", a=8))],
                    tag=f"su1{b}", reads=[f"u1fm{bp}"])
                pc_drip(5)
                if b % 4 == 3:
                    tt = b // 4
                    for m in range(16):
                        def f_z(e, m=m, tt=tt):
                            last = None
                            for k in range(8):
                                last = e.matmul(
                                    p1_mm[m % 2][:],
                                    inw_z_b[:, k * 2048 + m * 128:
                                            k * 2048 + (m + 1) * 128],
                                    h2sb[:, k * TOWN + tt * 512:
                                         k * TOWN + (tt + 1) * 512],
                                    start=(k == 0), stop=(k == 7))
                            return last
                        comp("tensor", f_z, tag=f"z{m}_{tt}",
                             reads=["h2sb", "w_inwz"], writes=[f"p1m{m % 2}"])
                        comp("scalar", lambda e, m=m: e.copy(
                            out=evac[:], in_=p1_mm[m % 2][:]),
                            tag=f"ze{m}_{tt}", reads=[f"p1m{m % 2}"],
                            writes=["evac"])
                        dma("sync", [(z_spill[m * 128:(m + 1) * 128,
                                              tt * 512:(tt + 1) * 512],
                                      evac[:])],
                            tag=f"zs{m}_{tt}", reads=["evac"])

            dma("sync", [(dap(ag_h2_in, 0,
                              [[TOWN, 128], [128 * TOWN, 8], [1, TOWN]]),
                          h2sb[:].rearrange("p (a t) -> p a t", a=8))],
                tag="sh2", reads=["h2sb"])
            ag_h2 = comp("gpsimd", lambda e: e.collective_compute(
                "AllGather", ALU.bypass, replica_groups=[list(range(NCORES))],
                ins=[ag_h2_in.ap().opt()], outs=[ag_h2_out.ap().opt()]),
                deps=[sc.steps[-1]], tag="ag_h2", coll=True)

            for r in range(NCORES):
                for lc in range(4):
                    pb = (r * 4 + lc) % 2
                    dma("sync", [(h2rd[:, pb * 4096:(pb + 1) * 4096].rearrange(
                        "p (a t) -> p a t", a=8),
                        dap(ag_h2_out, r * DM * TOWN + lc * 512,
                            [[TOWN, 128], [128 * TOWN, 8], [1, 512]]))],
                        deps=[ag_h2], tag=f"lh2_{r}_{lc}",
                        writes=[f"h2rd{pb}"])
                    for m in range(2):
                        def f_xs(e, lc=lc, m=m, pb=pb):
                            last = None
                            for k in range(8):
                                last = e.matmul(
                                    p1_mm[2 + m][:],
                                    inw_xs_b[:, k * 256 + m * 128:
                                             k * 256 + (m + 1) * 128],
                                    h2rd[:, pb * 4096 + k * 512:
                                         pb * 4096 + (k + 1) * 512],
                                    start=(k == 0), stop=(k == 7))
                            return last
                        comp("tensor", f_xs, tag=f"xs{r}_{lc}_{m}",
                             reads=[f"h2rd{pb}", "w_inwxs"],
                             writes=[f"p1m{2 + m}"])
                        ev = evac if m == 0 else evac2
                        comp("scalar", lambda e, m=m, ev=ev: e.copy(
                            out=ev[:], in_=p1_mm[2 + m][:]),
                            tag=f"xe{r}_{lc}_{m}", reads=[f"p1m{2 + m}"],
                            writes=[f"exv{m}"])
                        dst = dap(xs_spill,
                                  m * 128 * XSW + 4 + r * TOWN + lc * 512,
                                  [[XSW, 128], [1, 512]])
                        dma("sync", [(dst, ev[:])], tag=f"xss{r}_{lc}_{m}",
                            reads=[f"exv{m}"])
                    pc_drip(2)

            comp("vector", lambda e: e.memset(stat[0][:], 0.0), tag="zp",
                 writes=["stat0"])
            dma("sync", [
                (dap(xs_spill, 0, [[XSW, 128], [1, 4]]), stat[0][0:128, 0:4]),
                (dap(xs_spill, 128 * XSW, [[XSW, 128], [1, 4]]),
                 stat[0][0:128, 4:8]),
                (dap(xs_spill, T + 4, [[XSW, 128], [1, 4]]),
                 stat[0][0:128, 0:4]),
                (dap(xs_spill, 128 * XSW + T + 4, [[XSW, 128], [1, 4]]),
                 stat[0][0:128, 4:8]),
            ], tag="zps", reads=["stat0"])

        sc.barrier()

        st2 = ExitStack()
        with st2:
            ind32b = sb(st2, "ind32b_bf", (32, 32 * 128), BF16)
            dtf32 = sb(st2, "dtf32", (64, CH))
            dtbf = sb(st2, "dtbf", (64, CH), BF16)
            bcrows = sb(st2, "bcrows", (32, CH))
            bcbf = sb(st2, "bcbf", (32, CH), BF16)
            CW = CCH + 8
            xschs = [sb(st2, f"xsch{i}", (128, 2 * CW)) for i in range(2)]
            xcch = sb(st2, "xcch", (128, 2 * CCH))
            xcchs = [xcch, xcch]
            xcbf = sb(st2, "xcbf", (128, 2 * CCH), BF16)
            dtsb = sb(st2, "dtsb", (128, 2 * CH))
            csb = sb(st2, "csb", (128, 2 * CH), BF16)
            spsb = sb(st2, "spsb", (128, 2 * CH))
            vsb = sb(st2, "vsb", (128, 2 * CH))
            xcsc = sb(st2, "xcsc", (128, 2 * CH))
            dAsb = sb(st2, "dAsb", (128, 4 * CH))
            dBsb = sb(st2, "dBsb", (128, 4 * CH))
            hsb = sb(st2, "hsb", (128, 4 * CH))
            gsb = sb(st2, "gsb", (128, 4 * CH), BF16)
            ysb = sb(st2, "ysb", (128, 2 * CH), BF16)
            statev = sb(st2, "statev", (128, 2 * NST))
            p_bc = [psum(st2, f"p_bc{i}", (128, CH)) for i in range(2)]
            p_y = [psum(st2, f"p_y{i}", (128, CH)) for i in range(2)]

            for hh in range(2):
                dma("sync", [(wstg[0:32, 0:2048],
                              ind32_f[:, hh * 2048:(hh + 1) * 2048])],
                    tag=f"lind{hh}", writes=["wstg"])
                comp("vector", lambda e, hh=hh: e.tensor_copy(
                    out=ind32b[:, hh * 2048:(hh + 1) * 2048],
                    in_=wstg[0:32, 0:2048]),
                    tag=f"cind{hh}", reads=["wstg"], writes=["ind32bb"])

            arsteps = {}
            for d in range(2):
                xps_steps = []
                for cc in range(T // CCH):
                    par = (d * 8 + cc) % 2
                    xs_c = xschs[par]
                    dma("gpsimd", [
                        (xs_c[:, 0:CW],
                         dap(xs_spill, cc * CCH, [[XSW, 128], [1, CW]])),
                        (xs_c[:, CW:2 * CW],
                         dap(xs_spill, 128 * XSW + cc * CCH,
                             [[XSW, 128], [1, CW]])),
                    ], tag=f"lxs{d}_{cc}", writes=[f"xsch{par}"])

                    xc_c = xcchs[par]

                    def f_conv(e, d=d, xs_c=xs_c, xc_c=xc_c):
                        last = None
                        for k in range(4):
                            if k > 0:
                                e.drain()
                            for tl in range(2):
                                base = tl * CW
                                off = (1 + k) if d == 0 else (7 - k)
                                wcol = convw[:, d * 8 + tl * 4 + k:
                                             d * 8 + tl * 4 + k + 1]
                                win = xs_c[:, base + off:base + off + CCH]
                                o = xc_c[:, tl * CCH:(tl + 1) * CCH]
                                if k == 0:
                                    last = e.tensor_scalar_mul(
                                        out=o, in0=win, scalar1=wcol)
                                else:
                                    last = e.scalar_tensor_tensor(
                                        out=o, in0=win, scalar=wcol, in1=o,
                                        op0=ALU.mult, op1=ALU.add)
                        return last
                    comp("vector", f_conv, tag=f"cv{d}_{cc}",
                         reads=[f"xsch{par}", "smallw"], writes=["xcch"])

                    def f_silu(e, d=d, xc_c=xc_c):
                        last = None
                        for tl in range(2):
                            last = e.activation(
                                out=xc_c[:, tl * CCH:(tl + 1) * CCH],
                                in_=xc_c[:, tl * CCH:(tl + 1) * CCH],
                                func=AF.Silu,
                                bias=convb[:, d * 2 + tl:d * 2 + tl + 1])
                        return last
                    comp("scalar", f_silu, tag=f"si{d}_{cc}",
                         reads=["xcch", "smallw"], writes=["xcch"])
                    dma("sync", [(xc_spill[d][0:128, cc * CCH:(cc + 1) * CCH],
                                  xc_c[:, 0:CCH]),
                                 (xc_spill[d][128:256, cc * CCH:(cc + 1) * CCH],
                                  xc_c[:, CCH:2 * CCH])],
                        tag=f"sxc{d}_{cc}", reads=["xcch"])
                    comp("vector", lambda e, xc_c=xc_c: e.tensor_copy(
                        out=xcbf[:], in_=xc_c[:]),
                        tag=f"cb{d}_{cc}", reads=["xcch"],
                        writes=["xcbf"])
                    pc_drip(2)

                    for nt in range(CCH // 512):
                        def f_xp(e, d=d, nt=nt):
                            last = None
                            for tl in range(2):
                                last = e.matmul(
                                    p_bc[nt % 2][0:96, 0:512],
                                    xprojw_b[:, (d * 2 + tl) * 96:
                                             (d * 2 + tl + 1) * 96],
                                    xcbf[:, tl * CCH + nt * 512:
                                         tl * CCH + (nt + 1) * 512],
                                    start=(tl == 0), stop=(tl == 1))
                            return last
                        comp("tensor", f_xp, tag=f"xp{d}_{cc}_{nt}",
                             reads=["xcbf", "w_xp0", "w_xp1"],
                             writes=[f"p_bc{nt % 2}"])
                        comp("scalar", lambda e, nt=nt: e.copy(
                            out=evac[0:96, :], in_=p_bc[nt % 2][0:96, 0:512]),
                            tag=f"xpe{d}_{cc}_{nt}", reads=[f"p_bc{nt % 2}"],
                            writes=["evac"])
                        xps_steps.append(
                            dma("sync", [(ar_in[d][:, cc * CCH + nt * 512:
                                                   cc * CCH + (nt + 1) * 512],
                                          evac[0:96, :])],
                                tag=f"xps{d}_{cc}_{nt}", reads=["evac"]))

                arsteps[d] = comp(
                    "gpsimd", lambda e, d=d: e.collective_compute(
                        "AllReduce", ALU.add,
                        replica_groups=[list(range(NCORES))],
                        ins=[ar_in[d].ap().opt()], outs=[ar_out[d].ap().opt()]),
                    deps=[sc.steps[-1]] + xps_steps, tag=f"ar{d}", coll=True)

            for d in range(2):
                arstep = arsteps[d]
                chunks = list(range(NCH)) if d == 0 else \
                    list(range(NCH - 1, -1, -1))
                for ci, c in enumerate(chunks):
                    first = (ci == 0)
                    c0 = c * CH
                    dma("sync", [(dtf32[:, :], ar_out[d][0:64, c0:c0 + CH]),
                                 (bcrows[:, :], ar_out[d][64:96, c0:c0 + CH])],
                        deps=[arstep], tag=f"lxd{d}_{c}",
                        writes=["dtf32", "bcrows"])
                    def f_cxd(e):
                        e.tensor_copy(out=dtbf[:, :], in_=dtf32[:, :])
                        return e.tensor_copy(out=bcbf[:, :], in_=bcrows[:, :])
                    comp("vector", f_cxd, tag=f"cxd{d}_{c}",
                         reads=["dtf32", "bcrows"], writes=["dtbf", "bcbf"])
                    for tl in range(2):
                        for nt in range(2):
                            def f_dtp(e, d=d, tl=tl, nt=nt):
                                return e.matmul(
                                    p_y[tl][:, nt * 512:(nt + 1) * 512],
                                    dtw_b[0:64, (d * 2 + tl) * 128:
                                          (d * 2 + tl + 1) * 128],
                                    dtbf[0:64, nt * 512:(nt + 1) * 512],
                                    start=True, stop=True)
                            comp("tensor", f_dtp, tag=f"dtp{d}_{c}_{tl}_{nt}",
                                 reads=["dtbf", "w_dw0", "w_dw1"],
                                 writes=[f"p_y{tl}"])

                    def f_sp_s(e, d=d):
                        last = None
                        for tl in range(2):
                            bcol = dtbias[:, d * 2 + tl:d * 2 + tl + 1]
                            sl = spsb[:, tl * CH:(tl + 1) * CH]
                            e.activation(out=sl, in_=p_y[tl][:], func=AF.Abs,
                                         bias=bcol)
                            e.activation(out=sl, in_=sl, func=AF.Exp,
                                         scale=-1.0)
                            last = e.activation(out=sl, in_=sl, func=AF.Ln,
                                                bias=1.0)
                        return last
                    comp("scalar", f_sp_s, tag=f"sp{d}_{c}",
                         reads=["p_y0", "p_y1", "smallw"], writes=["spsb"])

                    def f_sp_v(e, d=d):
                        for tl in range(2):
                            bcol = dtbias[:, d * 2 + tl:d * 2 + tl + 1]
                            e.tensor_scalar(out=dtsb[:, tl * CH:(tl + 1) * CH],
                                            in0=p_y[tl][:], scalar1=bcol,
                                            scalar2=0.0, op0=ALU.add,
                                            op1=ALU.max)
                        e.drain()
                        last = None
                        for tl in range(2):
                            last = e.tensor_add(
                                out=dtsb[:, tl * CH:(tl + 1) * CH],
                                in0=dtsb[:, tl * CH:(tl + 1) * CH],
                                in1=spsb[:, tl * CH:(tl + 1) * CH])
                        return last
                    comp("vector", f_sp_v, tag=f"spv{d}_{c}",
                         reads=["p_y0", "p_y1", "spsb", "smallw"],
                         writes=["dtsb"])

                    dma("sync", [(xcsc[:, 0:CH],
                                  xc_spill[d][0:128, c0:c0 + CH]),
                                 (xcsc[:, CH:2 * CH],
                                  xc_spill[d][128:256, c0:c0 + CH])],
                        tag=f"lxc{d}_{c}", writes=["xcsc"])
                    comp("vector", lambda e: e.tensor_mul(
                        out=vsb[:], in0=dtsb[:], in1=xcsc[:]),
                        tag=f"v{d}_{c}", reads=["dtsb", "xcsc"],
                        writes=["vsb"])

                    def mk_bb(n):
                        def f_bb(e, n=n):
                            last = None
                            for half in range(2):
                                last = e.matmul(
                                    p_bc[0][:, half * 512:(half + 1) * 512],
                                    ind32b[:, n * 128:(n + 1) * 128],
                                    bcbf[:, half * 512:(half + 1) * 512],
                                    start=True, stop=True)
                            return last
                        comp("tensor", f_bb, tag=f"bb{d}_{c}_{n}",
                             reads=["bcbf", "ind32bb"], writes=["p_bc0"])

                        def f_cb2(e, n=n):
                            last = None
                            for half in range(2):
                                last = e.matmul(
                                    p_bc[1][:, half * 512:(half + 1) * 512],
                                    ind32b[:, (16 + n) * 128:(17 + n) * 128],
                                    bcbf[:, half * 512:(half + 1) * 512],
                                    start=True, stop=True)
                            return last
                        comp("tensor", f_cb2, tag=f"cb{d}_{c}_{n}",
                             reads=["bcbf", "ind32bb"], writes=["p_bc1"])
                        comp("scalar", lambda e, n=n: e.copy(
                            out=csb[:, (n % 2) * CH:(n % 2 + 1) * CH],
                            in_=p_bc[1][:]),
                            tag=f"ec{d}_{c}_{n}", reads=["p_bc1"],
                            writes=[f"bC{n % 2}"])

                    mk_bb(0)
                    for n in range(NST):
                        nb = n % 2
                        slots = [nb * 2 + tl for tl in range(2)]
                        dAs = [dAsb[:, s * CH:(s + 1) * CH] for s in slots]
                        dBs = [dBsb[:, s * CH:(s + 1) * CH] for s in slots]
                        hhs = [hsb[:, s * CH:(s + 1) * CH] for s in slots]
                        ggs = [gsb[:, s * CH:(s + 1) * CH] for s in slots]

                        for tl in range(2):
                            def f_dA(e, d=d, tl=tl, n=n, dA=dAs[tl]):
                                return e.activation(
                                    out=dA,
                                    in_=dtsb[:, tl * CH:(tl + 1) * CH],
                                    func=AF.Exp,
                                    scale=negA[:, d * 32 + tl * 16 + n:
                                               d * 32 + tl * 16 + n + 1])
                            comp("scalar", f_dA, tag=f"dA{d}_{c}_{n}_{tl}",
                                 reads=["dtsb", "smallw"],
                                 writes=[f"bA{slots[tl]}"])

                        for tl in range(2):
                            def f_dB(e, tl=tl, dB=dBs[tl]):
                                return e.tensor_mul(
                                    out=dB,
                                    in0=vsb[:, tl * CH:(tl + 1) * CH],
                                    in1=p_bc[0][:])
                            comp("vector", f_dB, tag=f"dB{d}_{c}_{n}_{tl}",
                                 reads=["vsb", "p_bc0"],
                                 writes=[f"bB{slots[tl]}"])

                        if n + 1 < NST:
                            mk_bb(n + 1)

                        for tl in range(2):
                            def f_scan(e, d=d, tl=tl, n=n, dA=dAs[tl],
                                       dB=dBs[tl], hh=hhs[tl], first=first):
                                scol = statev[:, tl * NST + n:
                                              tl * NST + n + 1]
                                init = 0.0 if first else scol
                                if d == 0:
                                    return e.tensor_tensor_scan(
                                        out=hh, data0=dA, data1=dB,
                                        initial=init, op0=ALU.mult,
                                        op1=ALU.add)
                                return e.tensor_tensor_scan(
                                    out=hh[:, ::-1], data0=dA[:, ::-1],
                                    data1=dB[:, ::-1], initial=init,
                                    op0=ALU.mult, op1=ALU.add)
                            comp("vector", f_scan, tag=f"sn{d}_{c}_{n}_{tl}",
                                 reads=[f"bA{slots[tl]}", f"bB{slots[tl]}",
                                        "statev"],
                                 writes=[f"bh{slots[tl]}"])

                        for tl in range(2):
                            def f_st(e, d=d, tl=tl, n=n, hh=hhs[tl]):
                                scol = statev[:, tl * NST + n:
                                              tl * NST + n + 1]
                                src = hh[:, CH - 1:CH] if d == 0 \
                                    else hh[:, 0:1]
                                return e.copy(out=scol, in_=src)
                            comp("scalar", f_st, tag=f"sv{d}_{c}_{n}_{tl}",
                                 reads=[f"bh{slots[tl]}"], writes=["statev"])

                        for tl in range(2):
                            def f_g(e, gg=ggs[tl], hh=hhs[tl], nb=nb):
                                return e.tensor_mul(
                                    out=gg, in0=hh,
                                    in1=csb[:, nb * CH:(nb + 1) * CH])
                            comp("vector", f_g, tag=f"g{d}_{c}_{n}_{tl}",
                                 reads=[f"bh{slots[tl]}", f"bC{nb}"],
                                 writes=[f"bg{slots[tl]}"])

                        for tl in range(2):
                            def f_red(e, tl=tl, n=n, gg=ggs[tl]):
                                last = None
                                for half in range(2):
                                    last = e.matmul(
                                        p_y[tl][:, half * 512:
                                                (half + 1) * 512],
                                        identb[:],
                                        gg[:, half * 512:(half + 1) * 512],
                                        start=(n == 0), stop=(n == NST - 1))
                                return last
                            rw = [f"p_y{tl}"] if (n == 0 or n == NST - 1) \
                                else ()
                            comp("tensor", f_red, tag=f"rd{d}_{c}_{n}_{tl}",
                                 reads=[f"bg{slots[tl]}", "consts"],
                                 writes=rw)

                    def f_y(e, d=d):
                        last = None
                        for tl in range(2):
                            last = e.scalar_tensor_tensor(
                                out=ysb[:, tl * CH:(tl + 1) * CH],
                                in0=xcsc[:, tl * CH:(tl + 1) * CH],
                                scalar=dskip[:, d * 2 + tl:d * 2 + tl + 1],
                                in1=p_y[tl][:], op0=ALU.mult, op1=ALU.add)
                        return last
                    comp("vector", f_y, tag=f"y{d}_{c}",
                         reads=["xcsc", "p_y0", "p_y1", "smallw"],
                         writes=["ysb"])

                    pc_drip(3)
                    if d == 0:
                        dma("sync", [(yf_spill[0:128, c0:c0 + CH],
                                      ysb[:, 0:CH]),
                                     (yf_spill[128:256, c0:c0 + CH],
                                      ysb[:, CH:2 * CH])],
                            tag=f"sy{c}", reads=["ysb"])
                    else:
                        dma("sync", [(ag_y_in[c][256:384, :], ysb[:, 0:CH]),
                                     (ag_y_in[c][384:512, :],
                                      ysb[:, CH:2 * CH]),
                                     (ag_y_in[c][0:256, :],
                                      yf_spill[:, c0:c0 + CH])],
                            tag=f"syb{c}", reads=["ysb"])
                        comp("gpsimd", lambda e, c=c:
                             e.collective_compute(
                                 "AllGather", ALU.bypass,
                                 replica_groups=[list(range(NCORES))],
                                 ins=[ag_y_in[c].ap().opt()],
                                 outs=[ag_y_out[c].ap().opt()]),
                             deps=[sc.steps[-1]], tag=f"agy{c}",
                             coll=True)
            pc_drip(len(pcitems))
            sc.barrier()

        agy = {int(s.tag[3:]): s for s in sc.steps if s.tag.startswith("agy")}
        st3 = ExitStack()
        with st3:
            ycsb = [sb(st3, f"ycsb{i}", (128, 32 * 512), BF16)
                    for i in range(2)]
            mwb = [sb(st3, f"mwb{i}", (128, 32 * 128), BF16)
                   for i in range(2)]
            gosb = sb(st3, "gosb", (128, 16 * 512), BF16)
            zsb = sb(st3, "zsb", (128, 512))
            sizsb = sb(st3, "sizsb", (128, 512))
            outw_b = sb(st3, "outw_b", (128, 16 * 1024), BF16)
            u1rd = sb(st3, "u1rd", (128, 512))
            asb = sb(st3, "asb", (128, 8 * 512))
            atm = sb(st3, "atm", (128, DM))
            sqp = sb(st3, "sqp", (128, DM))
            statp = sb(st3, "statp", (128, 8))
            x1tm = sb(st3, "x1tm", (128, DM))
            xrld = sb(st3, "xrld", (128, DM))
            u2p = sb(st3, "u2p", (128, DM))
            xfbt = sb(st3, "xfbt", (128, 8 * BLK), BF16)
            p4_mm = [psum(st3, f"p4m{i}", (128, 512)) for i in range(4)]
            p4_tr = psum(st3, "p4t", (128, 128))

            cast_w("outw", out_w[:, :], outw_b, 16, 1024)

            def ln_blk(src_sb, dst_sb, mu_c, var_c, tag, rbuf, wbuf):
                def f_a(e):
                    e.reduce_sum(out=statp[:, mu_c:mu_c + 1], in_=src_sb,
                                 axis=AX.X)
                    e.drain()
                    e.tensor_scalar_mul(out=statp[:, mu_c:mu_c + 1],
                                        in0=statp[:, mu_c:mu_c + 1],
                                        scalar1=RD)
                    e.drain()
                    return e.tensor_scalar(out=dst_sb, in0=src_sb,
                                           scalar1=statp[:, mu_c:mu_c + 1],
                                           scalar2=None, op0=ALU.subtract)
                comp("vector", f_a, tag=f"{tag}a", reads=[rbuf],
                     writes=[wbuf, "statp"])
                comp("scalar", lambda e: e.activation(out=sqp[:], in_=dst_sb,
                                                      func=AF.Square),
                     tag=f"{tag}s", reads=[wbuf], writes=["sqp"])

                def f_v(e):
                    e.reduce_sum(out=statp[:, var_c:var_c + 1], in_=sqp[:],
                                 axis=AX.X)
                    e.drain()
                    return e.tensor_scalar_mul(
                        out=statp[:, var_c:var_c + 1],
                        in0=statp[:, var_c:var_c + 1], scalar1=RD)
                comp("vector", f_v, tag=f"{tag}v", reads=["sqp"],
                     writes=["statp"])

                def f_r(e):
                    e.activation(out=statp[:, var_c + 1:var_c + 2],
                                 in_=statp[:, var_c:var_c + 1], func=AF.Ln,
                                 bias=epsc[:, 0:1])
                    return e.activation(out=statp[:, var_c + 2:var_c + 3],
                                        in_=statp[:, var_c + 1:var_c + 2],
                                        func=AF.Exp, scale=-0.5)
                comp("scalar", f_r, tag=f"{tag}r", reads=["statp"],
                     writes=["statp"])
                comp("vector", lambda e: e.tensor_scalar_mul(
                    out=dst_sb, in0=dst_sb,
                    scalar1=statp[:, var_c + 2:var_c + 3]),
                    tag=f"{tag}u", reads=[wbuf, "statp"], writes=[wbuf])

            for tt in range(4):
                tp_ = tt % 2
                lysem = {}

                def f_lyc(e, tt=tt, tp_=tp_, lysem=lysem):
                    last = None
                    for arm in e.switch_core_id(n=128):
                        j = arm.logical % 8
                        c = 2 * j + tt // 2
                        last = e.dma_start(
                            out=ycsb[tp_][:].rearrange("p (a t) -> p a t",
                                                       a=32),
                            in_=dap(ag_y_out[c], (tt % 2) * 512,
                                    [[1024, 128], [128 * 1024, 32],
                                     [1, 512]]),
                        ).then_inc(lysem["s"], 16)
                    return last
                s = sc._mk("sync", f_lyc,
                           [agy[c] for c in range(NCH)] if tt == 0 else [],
                           1, False, f"lyc{tt}", (), [f"ycsb{tp_}"])
                s.semref = lysem

                for m in range(16):
                    mp = m % 2
                    dma("sync", [(mwb[mp][:],
                                  dap(mw_bf, m * 128 * 4096,
                                      [[4096, 128], [1, 4096]]))],
                        tag=f"lmwb{tt}_{m}", writes=[f"mwb{mp}"])

                    def f_mg(e, m=m, mp=mp, tp_=tp_):
                        last = None
                        for k in range(32):
                            last = e.matmul(
                                p4_mm[m % 2][:],
                                mwb[mp][:, k * 128:(k + 1) * 128],
                                ycsb[tp_][:, k * 512:(k + 1) * 512],
                                start=(k == 0), stop=(k == 31))
                        return last
                    comp("tensor", f_mg, tag=f"mg{tt}_{m}",
                         reads=[f"mwb{mp}", f"ycsb{tp_}"],
                         writes=[f"p4m{m % 2}"])
                    dma("sync", [(zsb[:],
                                  z_spill[m * 128:(m + 1) * 128,
                                          tt * 512:(tt + 1) * 512])],
                        tag=f"lz{tt}_{m}", writes=["zsb"])
                    comp("scalar", lambda e: e.activation(
                        out=sizsb[:], in_=zsb[:], func=AF.Silu),
                        tag=f"sz{tt}_{m}", reads=["zsb"], writes=["sizsb"])
                    comp("vector", lambda e, m=m: e.tensor_mul(
                        out=gosb[:, m * 512:(m + 1) * 512], in0=sizsb[:],
                        in1=p4_mm[m % 2][:]),
                        tag=f"gt{tt}_{m}", reads=["sizsb", f"p4m{m % 2}"],
                        writes=["gosb"])

                for fo in range(8):
                    def f_ow(e, fo=fo):
                        last = None
                        for k in range(16):
                            last = e.matmul(
                                p4_mm[2 + fo % 2][:],
                                outw_b[:, k * 1024 + fo * 128:
                                       k * 1024 + (fo + 1) * 128],
                                gosb[:, k * 512:(k + 1) * 512],
                                start=(k == 0), stop=(k == 15))
                        return last
                    comp("tensor", f_ow, tag=f"ow{tt}_{fo}",
                         reads=["gosb", "w_outw"], writes=[f"p4m{2 + fo % 2}"])
                    dma("sync", [(u1rd[:],
                                  u1_spill[fo * 128:(fo + 1) * 128,
                                           tt * 512:(tt + 1) * 512])],
                        tag=f"lu1{tt}_{fo}", writes=["u1rd"])
                    comp("vector", lambda e, fo=fo: e.tensor_add(
                        out=asb[:, fo * 512:(fo + 1) * 512],
                        in0=u1rd[:],
                        in1=p4_mm[2 + fo % 2][:]),
                        tag=f"aa{tt}_{fo}",
                        reads=["u1rd", f"p4m{2 + fo % 2}"], writes=["asb"])

                for sb_ in range(4):
                    b = tt * 4 + sb_
                    for ft in range(8):
                        def f_ta(e, sb_=sb_, ft=ft):
                            return e.transpose(
                                p4_tr[:],
                                asb[:, ft * 512 + sb_ * 128:
                                    ft * 512 + (sb_ + 1) * 128], identf[:])
                        comp("tensor", f_ta, tag=f"ta{b}_{ft}",
                             reads=["asb", "consts"], writes=["p4t"])
                        comp("scalar", lambda e, ft=ft: e.copy(
                            out=atm[:, ft * 128:(ft + 1) * 128], in_=p4_tr[:]),
                            tag=f"ea{b}_{ft}", reads=["p4t"], writes=["atm"])

                    dma("sync", [(xrld[:], x_own[b * BLK:(b + 1) * BLK, :])],
                        tag=f"lx2{b}", writes=["xrld"])
                    ln_blk(atm[:], x1tm[:], 0, 1, f"l3_{b}", "atm", "x1tm")

                    def f_x1(e):
                        return e.tensor_add(out=x1tm[:], in0=x1tm[:],
                                            in1=xrld[:])
                    comp("vector", f_x1, tag=f"x1{b}",
                         reads=["x1tm", "xrld"], writes=["x1tm"])
                    dma("sync", [(x1_spill[b * BLK:(b + 1) * BLK, :],
                                  x1tm[:])], tag=f"sx1{b}", reads=["x1tm"])

                    ln_blk(x1tm[:], u2p[:], 4, 5, f"l4_{b}", "x1tm", "u2p")
                    for ft in range(8):
                        def f_tu2(e, ft=ft):
                            return e.transpose(
                                p4_tr[:], u2p[:, ft * 128:(ft + 1) * 128],
                                identf[:])
                        comp("tensor", f_tu2, tag=f"tu2{b}_{ft}",
                             reads=["u2p", "consts"], writes=["p4t"])

                        def f_exf(e, ft=ft):
                            return e.copy(out=xfbt[:, ft * BLK:(ft + 1) * BLK],
                                          in_=p4_tr[:])
                        comp("scalar", f_exf, tag=f"exf{b}_{ft}",
                             reads=["p4t"], writes=["xfbt"])
                    dma("sync", [(dap(xf_spill, b * BLK,
                                      [[TOWN, 128], [128 * TOWN, 8],
                                       [1, BLK]]),
                                  xfbt[:].rearrange("p (a t) -> p a t", a=8))],
                        tag=f"sxf{b}", reads=["xfbt"])

        sc.barrier()

        st4 = ExitStack()
        with st4:
            w1t = sb(st4, "w1t", (128, 8 * 2048), BF16)
            w2t = sb(st4, "w2t", (128, 16 * 1024), BF16)
            hesb = sb(st4, "hesb", (128, 16 * 512), BF16)
            hefs = sb(st4, "hefs", (128, 512))
            xfh = sb(st4, "xfh", (128, 8 * 1024), BF16)
            faccs = sb(st4, "faccs", (128, 8 * 1024))
            lgsb = sb(st4, "lgsb", (8, 1024))
            lgtm = sb(st4, "lgtm", (128, 8))
            etm = sb(st4, "etm", (128, 8))
            gatm = sb(st4, "gatm", (128, 8))
            msk = sb(st4, "msk", (128, 8))
            gfm = sb(st4, "gfm", (8, 1024))
            gfmb = sb(st4, "gfmb", (8, 1024), BF16)
            sstat = sb(st4, "sstat", (128, 4))
            gw2b = sb(st4, "gw2b", (128, 16 * 8), BF16)
            be2b = sb(st4, "be2b", (8, DM), BF16)
            gb1sb = sb(st4, "gb1sb", (128, 16))
            bs1sb = sb(st4, "bs1sb", (128, 16))
            bs2sb = sb(st4, "bs2sb", (128, 8))
            be1sb = sb(st4, "be1sb", (128, 8 * 16))
            ftm = sb(st4, "ftm", (128, DM))
            x2tm = sb(st4, "x2tm", (128, DM))
            x1rd = sb(st4, "x1rd", (128, DM))
            sqf = sb(st4, "sqf", (128, DM))
            statf = sb(st4, "statf", (128, 8))
            ind8 = sb(st4, "ind8", (8, 8 * 128))
            p_mm = [psum(st4, f"p5m{i}", (128, 512)) for i in range(4)]
            p_gb = psum(st4, "p5g", (128, 512))
            p_tr = psum(st4, "p5t", (128, 128))

            dma("sync", [(ind8[:, :], ind32_f[0:8, 0:8 * 128])], tag="lind8",
                writes=["ind8b"])

            dma("sync", [(gb1sb[:], gb1_t[:, :]), (bs1sb[:], bs1_t[:, :]),
                         (bs2sb[:], bs2_t[:, :]),
                         (be1sb[:].rearrange("p (a b) -> p a b", a=8),
                          dap(be1_t, 0, [[16, 128], [2048, 8], [1, 16]]))],
                tag="lffb", writes=["ffb"])
            cast_w("gw2", gw2[:, :], gw2b, 16, 8)
            dma("sync", [(wstg[0:8, 0:DM], be2[:, :])], tag="lbe2",
                writes=["wstg"])
            comp("vector", lambda e: e.tensor_copy(out=be2b[:, :],
                                                   in_=wstg[0:8, 0:DM]),
                 tag="cbe2", reads=["wstg"], writes=["w_be2"])

            def stream_w1(name, srch, soff):
                for k in range(8):
                    dma("sync", [(w1t[:, k * 2048:(k + 1) * 2048],
                                  dap(srch, soff + k * 128 * 2048,
                                      [[2048, 128], [1, 2048]]))],
                        tag=f"lw1{name}{k}", writes=["w1t"])

            def stream_w2(name, srch, soff):
                for k in range(16):
                    dma("sync", [(w2t[:, k * 1024:(k + 1) * 1024],
                                  dap(srch, soff + k * 128 * 1024,
                                      [[1024, 128], [1, 1024]]))],
                        tag=f"lw2{name}{k}", writes=["w2t"])

            outsteps = []
            for hf in range(2):
                dma("sync", [(xfh[:].rearrange("p (a t) -> p a t", a=8),
                              dap(xf_spill, hf * 1024,
                                  [[TOWN, 128], [128 * TOWN, 8], [1, 1024]]))],
                    tag=f"lxf{hf}", writes=["xfh"])

                stream_w1(f"g{hf}", gw1_bf, 0)
                for t2 in range(2):
                    for m in range(16):
                        def f_m1(e, t2=t2, m=m):
                            last = None
                            for k in range(8):
                                last = e.matmul(
                                    p_mm[m % 2][:],
                                    w1t[:, k * 2048 + m * 128:
                                        k * 2048 + (m + 1) * 128],
                                    xfh[:, k * 1024 + t2 * 512:
                                        k * 1024 + (t2 + 1) * 512],
                                    start=(k == 0), stop=(k == 7))
                            return last
                        comp("tensor", f_m1, tag=f"gm1_{hf}_{t2}_{m}",
                             reads=["xfh", "w1t"], writes=[f"p5m{m % 2}"])
                        comp("scalar", lambda e, m=m: e.activation(
                            out=hesb[:, m * 512:(m + 1) * 512],
                            in_=p_mm[m % 2][:], func=AF.Gelu,
                            bias=gb1sb[:, m:m + 1]),
                            tag=f"gge_{hf}_{t2}_{m}",
                            reads=[f"p5m{m % 2}", "ffb"], writes=["hesb"])

                    def f_lg(e, t2=t2):
                        last = None
                        for k in range(16):
                            last = e.matmul(
                                p_mm[2][0:8, :],
                                gw2b[0:128, k * 8:(k + 1) * 8],
                                hesb[:, k * 512:(k + 1) * 512],
                                start=(k == 0), stop=(k == 15))
                        return last
                    comp("tensor", f_lg, tag=f"glg{hf}_{t2}",
                         reads=["hesb", "w_gw2"], writes=["p5m2"])
                    comp("vector", lambda e, t2=t2: e.tensor_scalar(
                        out=lgsb[0:8, t2 * 512:(t2 + 1) * 512],
                        in0=p_mm[2][0:8, :], scalar1=gb2s[0:8, 0:1],
                        scalar2=None, op0=ALU.add),
                        tag=f"glge{hf}_{t2}", reads=["p5m2", "consts"],
                        writes=["lgsb"])

                for lb in range(8):
                    def f_tl(e, lb=lb):
                        return e.transpose(
                            p_tr[0:128, 0:8],
                            lgsb[0:8, lb * BLK:(lb + 1) * BLK],
                            identf[0:8, 0:8])
                    comp("tensor", f_tl, tag=f"tl{hf}_{lb}",
                         reads=["lgsb", "consts"], writes=["p5t"])
                    comp("scalar", lambda e: e.copy(out=lgtm[:],
                                                    in_=p_tr[0:128, 0:8]),
                         tag=f"tle{hf}_{lb}", reads=["p5t"], writes=["lgtm"])

                    comp("vector", lambda e: e.reduce_max(
                        out=sstat[:, 0:1], in_=lgtm[:], axis=AX.X,
                        negate=True),
                        tag=f"sm1{hf}_{lb}", reads=["lgtm"],
                        writes=["sstat"])
                    comp("scalar", lambda e: e.activation(
                        out=etm[:], in_=lgtm[:], func=AF.Exp,
                        bias=sstat[:, 0:1]),
                        tag=f"sm2{hf}_{lb}", reads=["lgtm", "sstat"],
                        writes=["etm"])

                    def f_sm3(e):
                        e.reduce_sum(out=sstat[:, 1:2], in_=etm[:], axis=AX.X)
                        e.drain()
                        return e.reciprocal(out=sstat[:, 2:3],
                                            in_=sstat[:, 1:2])
                    comp("vector", f_sm3, tag=f"sm3{hf}_{lb}",
                         reads=["etm"], writes=["sstat"])

                    def f_top2(e):
                        e.tensor_scalar_mul(out=etm[:], in0=etm[:],
                                            scalar1=sstat[:, 2:3])
                        e.drain()
                        e.reduce_max(out=sstat[:, 0:1], in_=etm[:], axis=AX.X)
                        e.drain()
                        e.tensor_scalar(out=msk[:], in0=etm[:],
                                        scalar1=sstat[:, 0:1], scalar2=-1.0,
                                        op0=ALU.is_ge, op1=ALU.mult)
                        e.drain()
                        e.tensor_scalar(out=msk[:], in0=msk[:], scalar1=1.0,
                                        scalar2=None, op0=ALU.add)
                        e.drain()
                        e.tensor_mul(out=gatm[:], in0=etm[:], in1=msk[:])
                        e.drain()
                        e.reduce_max(out=sstat[:, 3:4], in_=gatm[:],
                                     axis=AX.X)
                        e.drain()
                        e.tensor_scalar(out=msk[:], in0=etm[:],
                                        scalar1=sstat[:, 3:4], scalar2=None,
                                        op0=ALU.is_ge)
                        e.drain()
                        return e.tensor_mul(out=gatm[:], in0=etm[:],
                                            in1=msk[:])
                    comp("vector", f_top2, tag=f"tp{hf}_{lb}",
                         reads=["etm", "sstat"], writes=["gatm", "sstat"])

                    def f_tg(e):
                        return e.transpose(p_tr[0:8, 0:128], gatm[:],
                                           identf[:])
                    comp("tensor", f_tg, tag=f"tg{hf}_{lb}",
                         reads=["gatm", "consts"], writes=["p5t"])
                    comp("scalar", lambda e, lb=lb: e.copy(
                        out=gfm[0:8, lb * BLK:(lb + 1) * BLK],
                        in_=p_tr[0:8, 0:128]),
                        tag=f"tge{hf}_{lb}", reads=["p5t"], writes=["gfm"])
                comp("vector", lambda e: e.tensor_copy(out=gfmb[:, :],
                                                       in_=gfm[0:8, :]),
                     tag=f"cgf{hf}", reads=["gfm"], writes=["gfmb"])


                def mlp(name, w1h, w1off, w2h, w2off, b1col, gated_e,
                        is_shared):
                    stream_w1(name, w1h, w1off)
                    stream_w2(name, w2h, w2off)
                    for t2 in range(2):
                        if gated_e is not None:
                            def f_gb(e, t2=t2, ge=gated_e):
                                return e.matmul(
                                    p_gb[:],
                                    ind8[0:8, ge * 128:(ge + 1) * 128],
                                    gfm[0:8, t2 * 512:(t2 + 1) * 512],
                                    start=True, stop=True)
                            comp("tensor", f_gb, tag=f"gb{name}{t2}",
                                 reads=["gfm", "ind8b"], writes=["p5g"])
                        for m in range(16):
                            def f_m1(e, t2=t2, m=m):
                                last = None
                                for k in range(8):
                                    last = e.matmul(
                                        p_mm[m % 2][:],
                                        w1t[:, k * 2048 + m * 128:
                                            k * 2048 + (m + 1) * 128],
                                        xfh[:, k * 1024 + t2 * 512:
                                            k * 1024 + (t2 + 1) * 512],
                                        start=(k == 0), stop=(k == 7))
                                return last
                            comp("tensor", f_m1, tag=f"m1{name}{t2}_{m}",
                                 reads=["xfh", "w1t"], writes=[f"p5m{m % 2}"])
                            if gated_e is None:
                                def f_ge(e, m=m, b1col=b1col):
                                    return e.activation(
                                        out=hesb[:, m * 512:(m + 1) * 512],
                                        in_=p_mm[m % 2][:],
                                        func=AF.Gelu_apprx_tanh,
                                        bias=b1col(m))
                                comp("scalar", f_ge, tag=f"ge{name}{t2}_{m}",
                                     reads=[f"p5m{m % 2}", "ffb"],
                                     writes=["hesb"])
                            else:
                                def f_ge(e, m=m, b1col=b1col):
                                    return e.activation(
                                        out=hefs[:], in_=p_mm[m % 2][:],
                                        func=AF.Gelu_apprx_tanh,
                                        bias=b1col(m))
                                comp("scalar", f_ge, tag=f"ge{name}{t2}_{m}",
                                     reads=[f"p5m{m % 2}", "ffb"],
                                     writes=["hefs"])
                                comp("vector", lambda e, m=m: e.tensor_mul(
                                    out=hesb[:, m * 512:(m + 1) * 512],
                                    in0=hefs[:], in1=p_gb[:]),
                                    tag=f"gm{name}{t2}_{m}",
                                    reads=["hefs", "p5g"], writes=["hesb"])
                        for fo in range(8):
                            def f_m2(e, fo=fo):
                                last = None
                                for k in range(16):
                                    last = e.matmul(
                                        p_mm[2 + fo % 2][:],
                                        w2t[:, k * 1024 + fo * 128:
                                            k * 1024 + (fo + 1) * 128],
                                        hesb[:, k * 512:(k + 1) * 512],
                                        start=(k == 0), stop=(k == 15))
                                return last
                            comp("tensor", f_m2, tag=f"m2{name}{t2}_{fo}",
                                 reads=["hesb", "w2t"],
                                 writes=[f"p5m{2 + fo % 2}"])
                            fsl = faccs[:, fo * 1024 + t2 * 512:
                                        fo * 1024 + (t2 + 1) * 512]
                            if is_shared:
                                comp("vector", lambda e, fo=fo, fsl=fsl:
                                     e.tensor_scalar(
                                         out=fsl, in0=p_mm[2 + fo % 2][:],
                                         scalar1=bs2sb[:, fo:fo + 1],
                                         scalar2=None, op0=ALU.add),
                                     tag=f"fa{name}{t2}_{fo}",
                                     reads=[f"p5m{2 + fo % 2}", "ffb"],
                                     writes=["faccs"])
                            else:
                                comp("vector", lambda e, fo=fo, fsl=fsl:
                                     e.tensor_add(out=fsl, in0=fsl,
                                                  in1=p_mm[2 + fo % 2][:]),
                                     tag=f"fa{name}{t2}_{fo}",
                                     reads=[f"p5m{2 + fo % 2}", "faccs"],
                                     writes=["faccs"])

                mlp(f"s{hf}", ws1_bf, 0, ws2_bf, 0,
                    lambda m: bs1sb[:, m:m + 1], None, True)
                for ee in range(8):
                    mlp(f"h{hf}e{ee}", we1_bf, ee * DM * 2048,
                        we2_bf, ee * 2048 * DM,
                        (lambda m, ee=ee:
                         be1sb[:, ee * 16 + m:ee * 16 + m + 1]),
                        ee, False)

                for fo in range(8):
                    for t2 in range(2):
                        def f_b2(e, fo=fo, t2=t2):
                            return e.matmul(
                                p_mm[fo % 2][:],
                                be2b[0:8, fo * 128:(fo + 1) * 128],
                                gfmb[0:8, t2 * 512:(t2 + 1) * 512],
                                start=True, stop=True)
                        comp("tensor", f_b2, tag=f"b2{hf}_{fo}_{t2}",
                             reads=["w_be2", "gfmb"], writes=[f"p5m{fo % 2}"])
                        fsl = faccs[:, fo * 1024 + t2 * 512:
                                    fo * 1024 + (t2 + 1) * 512]
                        comp("vector", lambda e, fsl=fsl, fo=fo:
                             e.tensor_add(out=fsl, in0=fsl,
                                          in1=p_mm[fo % 2][:]),
                             tag=f"b2a{hf}_{fo}_{t2}",
                             reads=[f"p5m{fo % 2}", "faccs"],
                             writes=["faccs"])

                for lb in range(8):
                    b = hf * 8 + lb
                    for ft in range(8):
                        def f_tf(e, lb=lb, ft=ft):
                            return e.transpose(
                                p_tr[:],
                                faccs[:, ft * 1024 + lb * BLK:
                                      ft * 1024 + (lb + 1) * BLK],
                                identf[:])
                        comp("tensor", f_tf, tag=f"tf{b}_{ft}",
                             reads=["faccs", "consts"], writes=["p5t"])
                        comp("scalar", lambda e, ft=ft: e.copy(
                            out=ftm[:, ft * 128:(ft + 1) * 128],
                            in_=p_tr[:]),
                            tag=f"ef{b}_{ft}", reads=["p5t"], writes=["ftm"])
                    dma("sync", [(x1rd[:],
                                  x1_spill[b * BLK:(b + 1) * BLK, :])],
                        tag=f"lx1{b}", writes=["x1rd"])

                    def f_a5(e):
                        e.reduce_sum(out=statf[:, 0:1], in_=ftm[:], axis=AX.X)
                        e.drain()
                        e.tensor_scalar_mul(out=statf[:, 0:1],
                                            in0=statf[:, 0:1], scalar1=RD)
                        e.drain()
                        return e.tensor_scalar(out=x2tm[:], in0=ftm[:],
                                               scalar1=statf[:, 0:1],
                                               scalar2=None,
                                               op0=ALU.subtract)
                    comp("vector", f_a5, tag=f"l5a{b}", reads=["ftm"],
                         writes=["x2tm", "statf"])
                    comp("scalar", lambda e: e.activation(
                        out=sqf[:], in_=x2tm[:], func=AF.Square),
                        tag=f"l5s{b}", reads=["x2tm"], writes=["sqf"])

                    def f_v5(e):
                        e.reduce_sum(out=statf[:, 1:2], in_=sqf[:], axis=AX.X)
                        e.drain()
                        return e.tensor_scalar_mul(out=statf[:, 1:2],
                                                   in0=statf[:, 1:2],
                                                   scalar1=RD)
                    comp("vector", f_v5, tag=f"l5v{b}", reads=["sqf"],
                         writes=["statf"])

                    def f_r5(e):
                        e.activation(out=statf[:, 2:3], in_=statf[:, 1:2],
                                     func=AF.Ln, bias=epsc[:, 0:1])
                        return e.activation(out=statf[:, 3:4],
                                            in_=statf[:, 2:3], func=AF.Exp,
                                            scale=-0.5)
                    comp("scalar", f_r5, tag=f"l5r{b}", reads=["statf"],
                         writes=["statf"])

                    def f_x2(e):
                        e.tensor_scalar_mul(out=x2tm[:], in0=x2tm[:],
                                            scalar1=statf[:, 3:4])
                        e.drain()
                        return e.tensor_add(out=x2tm[:], in0=x2tm[:],
                                            in1=x1rd[:])
                    comp("vector", f_x2, tag=f"x2{b}",
                         reads=["x2tm", "x1rd", "statf"], writes=["x2tm"])
                    s = dma("sync", [(out_own.ap()[b * BLK:(b + 1) * BLK, :],
                                      x2tm[:])], tag=f"so{b}",
                            reads=["x2tm"])
                    outsteps.append(s)

            for nm in debug_names:
                h, shape, dt = internals[nm]
                s = dma("gpsimd", [(dbg_outs[nm].ap()[:, :], h.ap()[:, :])],
                        deps=[sc.steps[-1]], tag=f"dbg_{nm}")
                outsteps.append(s)

            comp("sync", lambda e: e.nop(), deps=outsteps, tag="fin")

        coll_sems = {n: es.enter_context(nc.semaphore(n))
                     for n in sc.coll_names}
        with nc.Block() as block:
            block.sync(lambda eng: sc.emit_for(sems, coll_sems, "sync", eng))
            block.vector(lambda eng: sc.emit_for(sems, coll_sems, "vector",
                                                 eng))
            block.scalar(lambda eng: sc.emit_for(sems, coll_sems, "scalar",
                                                 eng))
            block.tensor(lambda eng: sc.emit_for(sems, coll_sems, "tensor",
                                                 eng))
            block.gpsimd(lambda eng: sc.emit_for(sems, coll_sems, "gpsimd",
                                                 eng))
    return nc


def _ind32():
    ind = np.zeros((32, 32 * 128), dtype=np.float32)
    for j in range(32):
        ind[j, j * 128:(j + 1) * 128] = 1.0
    return ind


def host_inputs(inputs):
    g = lambda k: np.asarray(inputs[k])
    x = np.ascontiguousarray(g("L0_attn_x")[0])
    xb = x.reshape(T // BLK, BLK, DM)
    import ml_dtypes
    shared = {
        "gamma_t": None, "beta_t": None,
    }

    def tcol(v):
        k = v.shape[0] // 128
        return np.ascontiguousarray(v.reshape(k, 128).T)

    in_w = g("L0_attn_in_w")
    merge_w = g("L0_attn_merge_w")
    perm = np.concatenate([
        np.concatenate([np.arange(r * DSL, (r + 1) * DSL),
                        np.arange(DIN + r * DSL, DIN + (r + 1) * DSL)])
        for r in range(NCORES)])
    merge_w_p = np.ascontiguousarray(
        merge_w[perm].reshape(32, 128, 16, 128).transpose(2, 1, 0, 3)
        .reshape(16 * 128, 32 * 128))

    shared = {
        "gamma_t": tcol(g("L0_attn_gamma")),
        "beta_t": tcol(g("L0_attn_beta")),
        "in_w_z": np.ascontiguousarray(in_w[:, DIN:]),
        "ident_f": np.eye(128, dtype=np.float32),
        "ident_b": np.eye(128, dtype=ml_dtypes.bfloat16),
        "ones_f": np.ones((1, 128), dtype=np.float32),
        "ind32_f": _ind32(),
        "merge_w_p": merge_w_p,
        "out_w": g("L0_attn_out_w"),
        "gw1": g("L0_ffn_gw1"), "gb1_t": tcol(g("L0_ffn_gb1")),
        "gw2": g("L0_ffn_gw2"),
        "gb2c": np.ascontiguousarray(g("L0_ffn_gb2").reshape(8, 1)),
        "ws1": g("L0_ffn_Ws1"), "bs1_t": tcol(g("L0_ffn_bs1")),
        "ws2": g("L0_ffn_Ws2"), "bs2_t": tcol(g("L0_ffn_bs2")),
        "we1": g("L0_ffn_We1"),
        "be1_t": np.ascontiguousarray(
            g("L0_ffn_be1").reshape(8, 16, 128).transpose(0, 2, 1)),
        "we2": g("L0_ffn_We2"), "be2": g("L0_ffn_be2"),
    }

    def dsl_(v, j):
        return np.ascontiguousarray(v[j * DSL:(j + 1) * DSL])

    def dtile(v, j):
        return np.ascontiguousarray(dsl_(v, j).reshape(2, 128).T)

    in_maps = []
    for j in range(NCORES):
        m = dict(shared)
        m["x_own"] = np.ascontiguousarray(x[j * TOWN:(j + 1) * TOWN])
        m["in_w_xs"] = np.ascontiguousarray(in_w[:, j * DSL:(j + 1) * DSL])
        for d, sfx in ((0, ""), (1, "_b")):
            m[f"conv_w_t{d}"] = np.ascontiguousarray(
                dsl_(g(f"L0_attn_conv_w{sfx}"), j).reshape(
                    2, 128, 4).transpose(1, 0, 2))
            m[f"conv_b_t{d}"] = dtile(g(f"L0_attn_conv_b{sfx}"), j)
            m[f"dskip_t{d}"] = dtile(g(f"L0_attn_D_skip{sfx}"), j)
            m[f"dtb_t{d}"] = dtile(g(f"L0_attn_dt_b{sfx}"), j)
            m[f"alog_t{d}"] = np.ascontiguousarray(
                dsl_(g(f"L0_attn_A_log{sfx}"), j).reshape(
                    2, 128, NST).transpose(1, 0, 2))
            m[f"xproj_w{d}"] = dsl_(g(f"L0_attn_xproj_w{sfx}"), j)
            m[f"dt_w{d}"] = np.ascontiguousarray(
                g(f"L0_attn_dt_w{sfx}")[:, j * DSL:(j + 1) * DSL])
        in_maps.append(m)
    return in_maps


def run(inputs, debug_names=(), **kw):
    in_maps = host_inputs(inputs)
    nc = build_program(debug_names=tuple(debug_names))
    res = run_bass_kernel_spmd(nc, in_maps, core_ids=list(range(NCORES)), **kw)
    out = np.concatenate([res.results[j]["out_own"] for j in range(NCORES)],
                         axis=0)
    return np.ascontiguousarray(out).reshape(1, T, DM), res


def kernel(**inputs):
    out, _ = run(inputs)
    return out
